# version 27
# baseline (speedup 1.0000x reference)
import numpy as np
import ml_dtypes
import contextlib
import concourse.bass as bass
import concourse.mybir as mybir
from concourse.bass_utils import run_bass_kernel_spmd

F32 = mybir.dt.float32
BF16 = mybir.dt.bfloat16
AX = mybir.AxisListType.X
OP = mybir.AluOpType
AF = mybir.ActivationFunctionType

S = 2048
D = 1024
NT = 16
ND = 8
NC4 = 4
EPS = 1e-5
VW = 1025

DG_WV = 16 * 12
DG_E2SPLIT = 16 * 13
DSY_PRO = 48
T_QM, T_KM, T_E2 = 1, 2, 3
T_VP0 = 3
T_PSK = 20
T_SA0 = 20
T_PACKT = 37
V_ONES = 1
V_VHONES = 2
V_VE0 = 2
V_SSINK = 19
V_M0 = 19
V_PACK = 36
V_PACKT = 37
SC_QM, SC_KM, SC_E2, SC_PSINK = 1, 2, 3, 4
T0B, SC0B = T_PACKT, SC_PSINK
TB = lambda tt: T0B + tt + 1
SCB = lambda tt: SC0B + tt + 1
T0C, V0C, SC0C = T0B + NT, V_PACKT, SC0B + NT
TC = lambda j: T0C + j + 1
VC_RL = lambda j: V0C + 3 * j + 1
VC_Y = lambda j: V0C + 3 * j + 2
VC_R1 = lambda j: V0C + 3 * j + 3
DX_X = lambda j: 16 * (j // 2 + 1)
DX_WT = lambda c: 128 + 16 * (c // 2 + 1)
DX_OUT = lambda j: 192 + 16 * (j // 2 + 1)
DX_FINAL = 192 + 16 * (NT // 2)
T0D, V0D, SC0D, DG0D = T0C + NT, V0C + 3 * NT, SC0C + 2 * NT, DG_E2SPLIT
T_A, T_B = T0D + 1, T0D + 2
T_CT = lambda c: T0D + 2 + c + 1
V_A, V_B = V0D + 1, V0D + 2
V_WEFF = lambda c: V0D + 2 + c + 1
_TORDER = ([("tp", 0), ("tp", 1), ("A", 0), ("tp", 2), ("B", 0), ("tp", 3)]
           + [x for c in range(ND) for x in [("ct", c), ("tp", c + 4)]]
           + [("tp", 12), ("tp", 13), ("ffn", 0), ("tp", 14), ("ffn", 1), ("tp", 15)]
           + [("ffn", j) for j in range(2, NT)])
_T_TP = [0] * NT
_T_FO = [0] * NT
_T_CT = [0] * ND
_cnt = T0D
T_A = T_B = T_FOH15 = 0
for _k, _j in _TORDER:
    _cnt += 1
    if _k == "tp":
        _T_TP[_j] = _cnt
    elif _k == "ffn":
        if _j == NT - 1:
            T_FOH15 = _cnt
            _cnt += 1
        _T_FO[_j] = _cnt
    elif _k == "ct":
        _T_CT[_j] = _cnt
    elif _k == "A":
        T_A = _cnt
    else:
        T_B = _cnt
T_TP = lambda j: _T_TP[j]
T_FO = lambda j: _T_FO[j]
T_CT = lambda c: _T_CT[c]
V_A, V_B = V0D + 1, V0D + 2
V_WEFF = lambda c: V0D + 2 + c + 1
_V_Z = [0] * NT
_V_R2 = [0] * NT
_cnt = V0D + 10
for _j in range(NT):
    _cnt += 1; _V_Z[_j] = _cnt
    _cnt += 1; _V_R2[_j] = _cnt
V_Z = lambda j: _V_Z[j]
V_R2 = lambda j: _V_R2[j]
_SORDER = []
for _j in range(NT - 1):
    _SORDER += [("o1", _j), ("sq", _j)]
_SORDER += [("tp", 0), ("o1", NT - 1), ("tp", 1), ("sq", NT - 1)]
_SORDER += [("tp", _j) for _j in range(2, 14)]
for _j in range(NT):
    _SORDER.append(("ln", _j))
    if _j + 14 < NT:
        _SORDER.append(("tp", _j + 14))
_SC_O1 = [0] * NT
_SC_SQ = [0] * NT
_SC_TP = [0] * NT
_SCD_SQ = [0] * NT
_SCD_OUT = [0] * NT
_cnt = SC0C
for _k, _j in _SORDER:
    if _k == "o1":
        _cnt += 1; _SC_O1[_j] = _cnt
    elif _k == "sq":
        _cnt += 1; _SC_SQ[_j] = _cnt
    elif _k == "tp":
        _cnt += 1; _SC_TP[_j] = _cnt
    else:
        _cnt += 1; _SCD_SQ[_j] = _cnt
        _cnt += 1; _SCD_OUT[_j] = _cnt
SCC_O1 = lambda j: _SC_O1[j]
SCC_SQ = lambda j: _SC_SQ[j]
SC_TP = lambda j: _SC_TP[j]
SCD_SQ = lambda j: _SCD_SQ[j]
SCD_OUT = lambda j: _SCD_OUT[j]
DLP_PACKT = 16
DGP_PACKT = 16
DGP_ALL = 80


def build_nc(debug=False):
    nc = bass.Bass("TRN2", target_bir_lowering=False)

    x_d = nc.dram_tensor("x", [S, D], F32, kind="ExternalInput")
    xT_d = nc.dram_tensor("xT", [D, S], F32, kind="ExternalInput")
    wqkT_d = nc.dram_tensor("wqkT", [D, 258], F32, kind="ExternalInput")
    bcols_d = nc.dram_tensor("bcols", [128, 3], F32, kind="ExternalInput")
    wvT_d = nc.dram_tensor("wvT", [D, D], F32, kind="ExternalInput")
    bvb_d = nc.dram_tensor("bvb", [128, D], F32, kind="ExternalInput")
    k0m_d = nc.dram_tensor("k0m", [128, 1], F32, kind="ExternalInput")
    k0e_d = nc.dram_tensor("k0e", [1, 1], F32, kind="ExternalInput")
    v0r_d = nc.dram_tensor("v0r", [1, D], F32, kind="ExternalInput")
    WT_d = nc.dram_tensor("WT", [D, D], F32, kind="ExternalInput")
    lpack_d = nc.dram_tensor("lpack", [128, 2112], F32, kind="ExternalInput")
    onesp_d = nc.dram_tensor("onesp", [128, S], BF16, kind="ExternalInput")
    browp_d = nc.dram_tensor("browp", [128, D], F32, kind="ExternalInput")
    idf_d = nc.dram_tensor("idf_in", [128, 128], F32, kind="ExternalInput")
    out_d = nc.dram_tensor("out", [S, D], F32, kind="ExternalOutput")
    if debug:
        dbg = {nm: nc.dram_tensor("dbg_" + nm, shp, F32, kind="ExternalOutput")
               for nm, shp in [("mcols", [128, NT]), ("ssink", [128, NT]),
                                ("qe2", [2, S]), ("ke2", [2, S]), ("ptsink", [1, S]),
                                ("qmt", [128, S]), ("kmt", [128, S]),
                                ("pt0", [128, S]), ("x1f", [128, NT * D]),
                                ("packtf", [32, 128]), ("vh0", [128, VW]),
                                ("weff0", [128, D]), ("x1t0", [128, S])]}

    es = contextlib.ExitStack()
    sb = lambda name, shape, dt: es.enter_context(nc.sbuf_tensor(name, shape, dt))

    with es:
        QmT = sb("QmT", [128, S], BF16)
        KmT = sb("KmT", [128, S], BF16)
        Qe2 = sb("Qe2", [128, S], BF16)
        Ke2 = sb("Ke2", [128, S], BF16)
        Vh = sb("Vh", [128, NT * VW], BF16)
        Vs = sb("Vs", [128, VW], BF16)
        PTsink = sb("PTsink", [128, S], BF16)
        k0mb = sb("k0mb", [128, 1], BF16)
        k0eb = sb("k0eb", [1, 1], BF16)
        ssink = sb("ssink", [128, NT], F32)
        mcols = sb("mcols", [128, NT], F32)
        negmb = sb("negmb", [128, NT], BF16)
        psd = sb("psd", [128, NT], F32)
        packf = sb("packf", [128, 32], F32)
        packTf = sb("packTf", [32, 128], BF16)
        idf = sb("idf", [128, 128], F32)
        X1f = sb("X1f", [128, NT * D], F32)
        xs = [sb(f"xs{i}", [128, D], F32) for i in range(2)]
        o1s = [sb(f"o1s{i}", [128, D], F32) for i in range(2)]
        st12 = [sb(f"st12{i}", [128, 12], F32) for i in range(2)]
        smalls = sb("smalls", [128, 16], F32)
        tmpm = smalls[:, 0:1]
        sigp = [smalls[:, 1 + i:2 + i] for i in range(2)]
        sigs = [smalls[:, 3 + i:4 + i] for i in range(2)]
        r1p = [smalls[:, 5 + i:6 + i] for i in range(2)]
        nmrp = [smalls[:, 7 + i:8 + i] for i in range(2)]
        rls = [smalls[:, 9 + i:10 + i] for i in range(2)]
        epsc = smalls[:, 11:12]
        modd = sb("modd", [128, NT], F32)
        mvp2 = sb("mvp2", [128, 4], F32)
        mvp = [mvp2[:, 2 * i:2 * i + 2] for i in range(2)]

        dg = es.enter_context(nc.semaphore("dg"))
        dsy = es.enter_context(nc.semaphore("dsy"))
        t = es.enter_context(nc.semaphore("t"))
        v = es.enter_context(nc.semaphore("v"))
        sc = es.enter_context(nc.semaphore("sc"))
        dgc = [es.enter_context(nc.semaphore(f"dgc{i}")) for i in range(ND)]
        dx = [es.enter_context(nc.semaphore(f"dx{i}")) for i in range(2)]
        dlp = es.enter_context(nc.semaphore("dlp"))
        dgp = es.enter_context(nc.semaphore("dgp"))

        AR = sb("AR", [128, 32768], BF16)
        wqkb = AR[:, 0:ND * 258]
        wvb = AR[:, 2064:2064 + ND * D]
        XT = AR[:, 10256:10256 + ND * S]
        PT = AR
        X1T = AR
        bvbs = X1f[:, 0:D]
        bcols = X1f[:, D:D + 3]
        ys = [sb(f"ys{i}", [128, D], F32) for i in range(2)]
        pq = es.enter_context(nc.psum_tensor("pq", [128, 2048], F32))
        pv = [es.enter_context(nc.psum_tensor(f"pv{i}", [128, 1024], F32)) for i in range(2)]
        PO = [pq[:, 0:1024], pq[:, 1024:2048]]
        PL = [pv[0][:, 0:512], pv[0][:, 512:1024]]
        TP0 = pv[1]
        FO = PO
        WeffT = Vh[:, 0:8192]
        Ab = Vh[0:4, 13376:14400]
        Bb = Vh[:, 14400:15424]
        WTs = xs
        dab = QmT[0:100, 0:1024]
        uab = QmT[:, 1024:2048]
        ldb = KmT[0:100, 0:4]
        lub = KmT[0:4, 4:54]
        bbr = KmT[:, 64:1088]
        onesS = Qe2[:, 0:2048]

        with nc.Block() as blk:
            @blk.gpsimd
            def _(g):
                for c in range(ND):
                    g.dma_start(wqkb[:, c * 258:(c + 1) * 258],
                                wqkT_d[c * 128:(c + 1) * 128, :]).then_inc(dgc[c], 16)
                    g.dma_start(XT[:, c * S:(c + 1) * S],
                                xT_d[c * 128:(c + 1) * 128, :]).then_inc(dgc[c], 16)
                for c in range(ND):
                    g.dma_start(wvb[:, c * D:(c + 1) * D],
                                wvT_d[c * 128:(c + 1) * 128, :]).then_inc(dg, 16)
                g.dma_start(k0mb[:, :], k0m_d[:, :]).then_inc(dg, 16)
                g.dma_start(k0eb[:, :], k0e_d[:, :]).then_inc(dg, 16)
                g.wait_ge(v, V_ONES)
                g.dma_start(Vs[0:1, 0:D], v0r_d[:, :]).then_inc(dg, 16)
                g.dma_start(Ke2[1:2, :], PTsink[0:1, :]).then_inc(dg, 16)
                g.wait_ge(sc, SC_E2)
                g.wait_ge(dg, DG_WV)
                g.dma_start(Ke2[0:1, :], Qe2[1:2, :]).then_inc(dg, 16)
                g.wait_ge(v, V_PACKT)
                g.dma_start(PTsink[0:1, :], packTf[16:32, :]).then_inc(dgp, 16)
                g.wait_ge(t, TB(NT - 1))
                g.dma_start(QmT[:, :], lpack_d[:, 0:2048]).then_inc(dgp, 16)
                g.dma_start(KmT[:, 0:64], lpack_d[:, 2048:2112]).then_inc(dgp, 16)
                g.dma_start(bbr, browp_d[:, :]).then_inc(dgp, 16)
                g.dma_start(onesS, onesp_d[:, :]).then_inc(dgp, 16)

            @blk.sync
            def _(sy):
                sy.dma_start(bvbs[:, :], bvb_d[:, :]).then_inc(dsy, 16)
                sy.dma_start(bcols[:, :], bcols_d[:, :]).then_inc(dsy, 16)
                sy.dma_start(idf[:, :], idf_d[:, :]).then_inc(dsy, 16)
                sy.wait_ge(v, V_PACKT)
                sy.dma_start(Qe2[1:2, :], packTf[0:16, :]).then_inc(dlp, 16)

                for j in range(NT):
                    if j >= 2:
                        sy.wait_ge(v, VC_Y(j - 2))
                    sy.dma_start(xs[j % 2][:, :],
                                 x_d[j * 128:(j + 1) * 128, :]).then_inc(dx[j % 2], 16)
                for c in range(ND):
                    if c == 0:
                        sy.wait_ge(v, VC_Y(NT - 2))
                    elif c == 1:
                        sy.wait_ge(v, VC_Y(NT - 1))
                    else:
                        sy.wait_ge(v, V_WEFF(c - 2))
                    sy.dma_start(WTs[c % 2][:, :],
                                 WT_d[c * 128:(c + 1) * 128, :]).then_inc(dx[c % 2], 16)
                for j in range(NT):
                    if j == NT - 1:
                        sy.wait_ge(sc, SCD_OUT(j))
                        sy.dma_start(out_d[j * 128:(j + 1) * 128, 0:512],
                                     o1s[j % 2][:, 0:512]).then_inc(dx[j % 2], 16)
                        sy.wait_ge(sc, SCD_OUT(j) + 1)
                        sy.dma_start(out_d[j * 128:(j + 1) * 128, 512:1024],
                                     o1s[j % 2][:, 512:1024]).then_inc(dx[j % 2], 16)
                    else:
                        sy.wait_ge(sc, SCD_OUT(j))
                        sy.dma_start(out_d[j * 128:(j + 1) * 128, :],
                                     o1s[j % 2][:, :]).then_inc(dx[j % 2], 16)
                sy.wait_ge(dx[0], DX_FINAL)
                sy.wait_ge(dx[1], DX_FINAL + 16)

            @blk.tensor
            def _(te):
                for c in range(ND):
                    te.wait_ge(dgc[c], 32)
                    for n in range(NC4):
                        n0, n1 = n * 512, (n + 1) * 512
                        te.matmul(pq[:, n0:n1],
                                  wqkb[:, c * 258: c * 258 + 128],
                                  XT[:, c * S + n0: c * S + n1],
                                  start=(c == 0), stop=(c == ND - 1))
                    for n in range(NC4):
                        n0, n1 = n * 512, (n + 1) * 512
                        outp = (pv[n // 2])[:, (n % 2) * 512:(n % 2 + 1) * 512]
                        te.matmul(outp,
                                  wqkb[:, c * 258 + 128: c * 258 + 256],
                                  XT[:, c * S + n0: c * S + n1],
                                  start=(c == 0), stop=(c == ND - 1))
                te.drain(fusable=True).then_inc(t, 1)
                te.drain(fusable=True).then_inc(t, 1)
                te.wait_ge(sc, SC_QM)
                for n in range(NC4):
                    n0, n1 = n * 512, (n + 1) * 512
                    for c in range(ND):
                        ins = te.matmul(pq[0:2, n0:n1],
                                        wqkb[:, c * 258 + 256: c * 258 + 258],
                                        XT[:, c * S + n0: c * S + n1],
                                        start=(c == 0), stop=(c == ND - 1))
                te.drain(fusable=True).then_inc(t, 1)
                for tt in range(NT):
                    if tt < 2:
                        te.wait_ge(sc, SC_KM)
                    else:
                        te.wait_ge(v, V_VE0 + (tt - 2) + 1)
                    for n in range(2):
                        n0, n1 = n * 512, (n + 1) * 512
                        for c in range(ND):
                            if tt == 0 and n == 0 and c == 0:
                                te.wait_ge(dg, DG_WV)
                            ins = te.matmul(pv[tt % 2][:, n0:n1],
                                            XT[:, c * S + tt * 128: c * S + (tt + 1) * 128],
                                            wvb[:, c * D + n0: c * D + n1],
                                            start=(c == 0), stop=(c == ND - 1))
                    te.drain(fusable=True).then_inc(t, 1)
                te.wait_ge(sc, SC_E2)
                te.wait_ge(dg, DG_E2SPLIT)
                for j in range(NT):
                    te.matmul(pq[:, j:j + 1], QmT[:, j * 128:(j + 1) * 128],
                              k0mb[:, :], start=True, stop=False)
                    ins = te.matmul(pq[:, j:j + 1], Qe2[0:1, j * 128:(j + 1) * 128],
                                    k0eb[:, :], start=False, stop=True)
                te.drain(fusable=True).then_inc(t, 1)
                for j in range(NT):
                    if j == 0:
                        te.wait_ge(v, V_SSINK)
                    elif j == 1:
                        te.wait_ge(v, V_VE0 + NT)
                    else:
                        te.wait_ge(v, V_M0 + (j - 2) + 1)
                    for n in range(NC4):
                        n0, n1 = n * 512, (n + 1) * 512
                        if j % 2 == 0:
                            outp = pq[:, n0:n1]
                        else:
                            outp = (pv[n // 2])[:, (n % 2) * 512:(n % 2 + 1) * 512]
                        te.matmul(outp, QmT[:, j * 128:(j + 1) * 128],
                                  KmT[:, n0:n1], start=True, stop=False)
                        te.matmul(outp, Qe2[:, j * 128:(j + 1) * 128],
                                  Ke2[:, n0:n1], start=False, stop=True)
                    te.drain(fusable=True).then_inc(t, 1)
                te.wait_ge(sc, SC_PSINK)
                te.wait_ge(dsy, DSY_PRO)
                te.matmul(pv[0][0:32, 0:128], packf[:, :], idf[:, :],
                          is_transpose=True, start=True, stop=True)
                te.drain(fusable=True).then_inc(t, 1)

                def sb5(tt, n):
                    if tt % 2 == 0:
                        return pq[:, n * 512:(n + 1) * 512]
                    return (pv[n // 2])[:, (n % 2) * 512:(n % 2 + 1) * 512]
                te.wait_ge(v, V_M0 + 15)
                for n in range(NC4):
                    te.matmul(sb5(0, n), KmT[:, 0:128],
                              QmT[:, n * 512:(n + 1) * 512], start=True, stop=False)
                te.wait_ge(v, V_PACKT)
                for n in range(NC4):
                    te.matmul(sb5(1, n), KmT[:, 128:256],
                              QmT[:, n * 512:(n + 1) * 512], start=True, stop=False)
                te.wait_ge(dlp, DLP_PACKT)
                te.wait_ge(dgp, DGP_PACKT)
                for tt in range(NT):
                    if tt >= 2:
                        te.wait_ge(sc, SCB(tt - 2))
                    for n in range(NC4):
                        n0, n1 = n * 512, (n + 1) * 512
                        if tt >= 2:
                            te.matmul(sb5(tt, n),
                                      KmT[:, tt * 128:(tt + 1) * 128],
                                      QmT[:, n0:n1], start=True, stop=False)
                        te.matmul(sb5(tt, n),
                                  Ke2[:, tt * 128:(tt + 1) * 128],
                                  Qe2[:, n0:n1], start=False, stop=True)
                    te.drain(fusable=True).then_inc(t, 1)

                te.wait_ge(dlp, DLP_PACKT)
                te.wait_ge(dgp, DGP_PACKT)
                te.wait_ge(v, V_PACKT)
                te.wait_ge(sc, SC0C)
                for j in range(NT):
                    if j >= 2:
                        te.wait_ge(sc, SCC_O1(j - 2))
                        te.wait_ge(v, VC_RL(j - 2))
                    for c in range(NT + 1):
                        first, last = (c == 0), (c == NT)
                        if c < NT:
                            lhs = PT[:, c * S + j * 128: c * S + (j + 1) * 128]
                            r1_ = Vh[:, c * VW: c * VW + 512]
                            r2_ = Vh[:, c * VW + 512: c * VW + 1024]
                            r3_ = Vh[:, c * VW + D: c * VW + VW]
                        else:
                            lhs = PTsink[:, j * 128:(j + 1) * 128]
                            r1_ = Vs[:, 0:512]
                            r2_ = Vs[:, 512:1024]
                            r3_ = Vs[:, D:VW]
                        te.matmul(PL[j % 2][:, 0:1], lhs, r3_, start=first, stop=last)
                        te.matmul(PO[j % 2][:, 0:512], lhs, r1_, start=first, stop=last)
                        te.matmul(PO[j % 2][:, 512:1024], lhs, r2_, start=first, stop=last)
                    te.drain(fusable=True).then_inc(t, 1)

                def emit_A():
                    te.wait_ge(dgp, DGP_ALL)
                    te.wait_ge(sc, SCC_O1(NT - 2))
                    for n in range(2):
                        te.matmul(FO[0][0:4, n * 512:(n + 1) * 512], ldb,
                                  dab[:, n * 512:(n + 1) * 512], start=True, stop=True)
                    te.drain(fusable=True).then_inc(t, 1)

                def emit_B():
                    te.wait_ge(v, V_A)
                    for n in range(2):
                        te.matmul(FO[0][0:50, n * 512:(n + 1) * 512], lub,
                                  Ab[:, n * 512:(n + 1) * 512], start=True, stop=True)
                    te.drain(fusable=True).then_inc(t, 1)

                def emit_ct(c):
                    if c == 1:
                        te.wait_ge(sc, SCC_O1(NT - 1))
                    te.wait_ge(v, V_B if c < 2 else V_WEFF(c - 2))
                    for n in range(2):
                        te.matmul(FO[c % 2][:, n * 512:(n + 1) * 512],
                                  Bb[:, c * 128:(c + 1) * 128],
                                  uab[:, n * 512:(n + 1) * 512], start=True, stop=True)
                    te.drain(fusable=True).then_inc(t, 1)

                def emit_tp(j):
                    te.wait_ge(v, VC_R1(j))
                    if j == 1:
                        te.wait_ge(v, VC_RL(NT - 1))
                    if j >= 2:
                        te.wait_ge(sc, SC_TP(j - 2))
                    for b in range(ND):
                        if j % 2 == 0:
                            outp = TP0[:, b * 128:(b + 1) * 128]
                        else:
                            outp = PL[b // 4][:, (b % 4) * 128:(b % 4 + 1) * 128]
                        te.matmul(outp,
                                  X1f[:, j * D + b * 128: j * D + (b + 1) * 128],
                                  idf[:, :], is_transpose=True,
                                  start=(b % 4 == 0), stop=(b % 4 == 3))
                    te.drain(fusable=True).then_inc(t, 1)

                def emit_ffn(j):
                    if j == 0:
                        te.wait_ge(dgp, DGP_ALL)
                    te.wait_ge(sc, SC_TP(j))
                    if j < 2:
                        te.wait_ge(v, V_WEFF(ND - 1))
                    else:
                        te.wait_ge(v, V_Z(j - 2))
                    for n in range(2):
                        n0, n1 = n * 512, (n + 1) * 512
                        for c in range(ND):
                            te.matmul(FO[j % 2][:, n0:n1],
                                      X1T[:, c * S + j * 128: c * S + (j + 1) * 128],
                                      WeffT[:, c * D + n0: c * D + n1],
                                      start=(c == 0), stop=False)
                        te.matmul(FO[j % 2][:, n0:n1],
                                  onesS[:, j * 128:(j + 1) * 128],
                                  bbr[:, n0:n1], start=False, stop=True)
                        if j == NT - 1 and n == 0:
                            te.drain(fusable=True).then_inc(t, 1)
                    te.drain(fusable=True).then_inc(t, 1)

                emitters = {"tp": emit_tp, "ffn": emit_ffn, "ct": emit_ct,
                            "A": lambda j: emit_A(), "B": lambda j: emit_B()}
                for k, j in _TORDER:
                    emitters[k](j)

            def emit_z(ve, j):
                p = j % 2
                if j == NT - 1:
                    ve.wait_ge(t, T_FOH15)
                    ve.tensor_tensor(xs[p][:, 0:512], X1f[:, j * D: j * D + 512],
                                     FO[p][:, 0:512], OP.add)
                    ve.drain()
                    ve.bn_stats(st12[p][:, 0:6], xs[p][:, 0:512])
                    ve.wait_ge(t, T_FO(j))
                    ve.tensor_tensor(xs[p][:, 512:1024],
                                     X1f[:, j * D + 512:(j + 1) * D],
                                     FO[p][:, 512:1024], OP.add)
                    ve.drain()
                    ve.bn_stats(st12[p][:, 6:12], xs[p][:, 512:1024])
                    ve.drain()
                else:
                    ve.wait_ge(t, T_FO(j))
                    ve.tensor_tensor(xs[p][:, 0:512], X1f[:, j * D: j * D + 512],
                                     FO[p][:, 0:512], OP.add)
                    ve.tensor_tensor(xs[p][:, 512:1024],
                                     X1f[:, j * D + 512:(j + 1) * D],
                                     FO[p][:, 512:1024], OP.add)
                    ve.drain()
                    ve.bn_stats(st12[p][:, 0:6], xs[p][:, 0:512])
                    ve.bn_stats(st12[p][:, 6:12], xs[p][:, 512:1024])
                    ve.drain()
                ve.bn_aggr(mvp[p], st12[p][:, :])
                ve.drain().then_inc(v, 1)
                ve.wait_ge(sc, SCD_SQ(j))
                ve.reciprocal(r1p[p], sigs[p])
                ve.drain()
                ve.tensor_scalar(nmrp[p], mvp[p][:, 0:1], r1p[p],
                                 -1.0, OP.mult, OP.mult)
                ve.drain().then_inc(v, 1)

            @blk.vector
            def _(ve):
                ve.memset(epsc, EPS)
                ve.memset(Qe2[:, :], 0.0)
                ve.memset(Ke2[:, :], 0.0)
                ve.memset(PTsink[:, :], 0.0)
                ve.memset(Vs[:, :], 0.0)
                ve.drain()
                ve.memset(PTsink[0:1, :], 1.0)
                ve.memset(Vs[0:1, D:VW], 1.0).then_inc(v, 1)
                ve.memset(bass.AP(Vh, D, [[NT * VW, 128], [VW, NT]]), 1.0).then_inc(v, 1)
                ve.wait_ge(dsy, DSY_PRO)
                for tt in range(NT):
                    ve.wait_ge(t, T_VP0 + tt + 1)
                    ve.tensor_tensor(Vh[:, tt * VW: tt * VW + D], pv[tt % 2][:, :],
                                     bvbs[:, :], OP.add).then_inc(v, 1)
                ve.wait_ge(t, T_PSK)
                ve.tensor_copy(ssink[:, :], pq[:, 0:NT]).then_inc(v, 1)
                for j in range(NT):
                    ve.wait_ge(t, T_SA0 + j + 1)
                    if j % 2 == 0:
                        ve.reduce_max(mcols[:, j:j + 1], pq[:, :], axis=AX).then_inc(v, 1)
                    else:
                        jo = j // 2
                        ve.reduce_max(modd[:, 2 * jo:2 * jo + 1], pv[0][:, :], axis=AX)
                        ve.reduce_max(modd[:, 2 * jo + 1:2 * jo + 2], pv[1][:, :],
                                      axis=AX).then_inc(v, 1)
                ve.drain()
                ve.tensor_tensor(bass.AP(mcols, 1, [[NT, 128], [2, 8]]),
                                 bass.AP(modd, 0, [[NT, 128], [2, 8]]),
                                 bass.AP(modd, 1, [[NT, 128], [2, 8]]), OP.max)
                ve.drain()
                ve.tensor_tensor(mcols[:, :], mcols[:, :], ssink[:, :], OP.max)
                ve.drain()
                ve.tensor_scalar_mul(negmb[:, :], mcols[:, :], -1.0)
                ve.drain()
                ve.tensor_tensor(psd[:, :], ssink[:, :], negmb[:, :], OP.add)
                ve.tensor_copy(packf[:, 0:16], negmb[:, :])
                ve.drain().then_inc(v, 1)
                ve.wait_ge(t, T_PACKT)
                ve.tensor_copy(packTf[:, :], pv[0][0:32, 0:128]).then_inc(v, 1)

                for j in range(NT):
                    p = j % 2
                    ve.wait_ge(t, TC(j))
                    ve.reciprocal(rls[p], PL[p][:, 0:1])
                    ve.drain().then_inc(v, 1)
                    ve.wait_ge(sc, SCC_O1(j))
                    ve.wait_ge(dx[j % 2], DX_X(j))
                    ve.scalar_tensor_tensor(ys[p][:, 0:512], xs[p][:, 0:512], 2.0,
                                            o1s[p][:, 0:512], OP.mult, OP.add)
                    ve.scalar_tensor_tensor(ys[p][:, 512:1024], xs[p][:, 512:1024], 2.0,
                                            o1s[p][:, 512:1024], OP.mult, OP.add)
                    ve.drain()
                    ve.bn_stats(st12[p][:, 0:6], ys[p][:, 0:512])
                    ve.bn_stats(st12[p][:, 6:12], ys[p][:, 512:1024])
                    ve.drain()
                    ve.bn_aggr(mvp[p], st12[p][:, :])
                    ve.drain().then_inc(v, 1)
                    ve.wait_ge(sc, SCC_SQ(j))
                    ve.reciprocal(r1p[p], sigs[p])
                    ve.drain()
                    ve.tensor_scalar(nmrp[p], mvp[p][:, 0:1], r1p[p],
                                     -1.0, OP.mult, OP.mult)
                    ve.drain()
                    ve.tensor_scalar(X1f[:, j * D:(j + 1) * D], ys[p][:, :],
                                     r1p[p], nmrp[p], OP.mult, OP.add)
                    ve.drain().then_inc(v, 1)
                ve.wait_ge(t, TC(NT - 1))
                ve.memset(Bb, 0.0)
                ve.wait_ge(t, T_A)
                ve.tensor_copy(Ab, FO[0][0:4, :])
                ve.drain().then_inc(v, 1)
                ve.wait_ge(t, T_B)
                ve.tensor_copy(Bb[0:50, :], FO[0][0:50, :])
                ve.drain().then_inc(v, 1)
                for c in range(ND):
                    ve.wait_ge(t, T_CT(c))
                    ve.wait_ge(dx[c % 2], DX_WT(c))
                    ve.tensor_tensor(WeffT[:, c * D:(c + 1) * D], FO[c % 2][:, :],
                                     WTs[c % 2][:, :], OP.add)
                    ve.drain().then_inc(v, 1)
                for j in range(NT):
                    emit_z(ve, j)

            @blk.scalar
            def _(s):
                s.wait_ge(v, V_ONES)
                s.activation(o1s[0][:, 0:1], epsc, AF.Identity,
                             bias=epsc, scale=1.0)
                s.activation(o1s[1][:, 0:1], epsc, AF.Copy)
                s.activation(ys[0][:, 0:1], epsc, AF.Exp)
                s.activation(ys[1][:, 0:1], epsc, AF.Sqrt, bias=epsc, scale=1.0)
                s.wait_ge(t, T_QM)
                s.wait_ge(v, V_ONES)
                s.wait_ge(dsy, DSY_PRO)
                s.activation(QmT[:, :], pq[:, :], AF.Identity,
                             bias=bcols[:, 0:1], scale=1.0).then_inc(sc, 1)
                s.wait_ge(t, T_KM)
                s.activation(KmT[:, 0:1024], pv[0][:, :], AF.Identity,
                             bias=bcols[:, 1:2], scale=1.0)
                s.activation(KmT[:, 1024:2048], pv[1][:, :], AF.Identity,
                             bias=bcols[:, 1:2], scale=1.0).then_inc(sc, 1)
                s.wait_ge(t, T_E2)
                s.activation(Qe2[0:2, :], pq[0:2, :], AF.Identity,
                             bias=bcols[0:2, 2:3], scale=1.0).then_inc(sc, 1)
                s.wait_ge(v, V_PACK)
                s.activation(packf[:, 16:32], psd[:, :], AF.Exp).then_inc(sc, 1)

                for tt in range(NT):
                    s.wait_ge(t, TB(tt))
                    if tt % 2 == 0:
                        s.activation(PT[:, tt * S:(tt + 1) * S], pq[:, :],
                                     AF.Exp).then_inc(sc, 1)
                    else:
                        s.activation(PT[:, tt * S: tt * S + 1024], pv[0][:, :], AF.Exp)
                        s.activation(PT[:, tt * S + 1024:(tt + 1) * S], pv[1][:, :],
                                     AF.Exp).then_inc(sc, 1)

                def s_o1(j):
                    p = j % 2
                    s.wait_ge(v, VC_RL(j))
                    s.activation(o1s[p][:, :], PO[p][:, :], AF.Identity,
                                 scale=rls[p]).then_inc(sc, 1)
                def s_sq(j):
                    p = j % 2
                    s.wait_ge(v, VC_Y(j))
                    s.activation(sigs[p], mvp[p][:, 1:2], AF.Sqrt,
                                 bias=epsc, scale=1.0).then_inc(sc, 1)
                def s_tp(j):
                    s.wait_ge(t, T_TP(j))
                    if j % 2 == 0:
                        s.activation(
                            bass.AP(PT, j * 128, [[NT * S, 128], [S, ND], [1, 128]]),
                            TP0[:, :], AF.Copy).then_inc(sc, 1)
                    else:
                        s.activation(
                            bass.AP(PT, j * 128, [[NT * S, 128], [S, 4], [1, 128]]),
                            PL[0][:, :], AF.Copy)
                        s.activation(
                            bass.AP(PT, 4 * S + j * 128,
                                    [[NT * S, 128], [S, 4], [1, 128]]),
                            PL[1][:, :], AF.Copy).then_inc(sc, 1)

                def s_ln(j):
                    p = j % 2
                    s.wait_ge(v, V_Z(j))
                    s.activation(sigs[p], mvp[p][:, 1:2], AF.Sqrt,
                                 bias=epsc, scale=1.0).then_inc(sc, 1)
                    s.wait_ge(v, V_R2(j))
                    if j >= 2:
                        s.wait_ge(dx[j % 2], DX_OUT(j - 2))
                    if j == NT - 1:
                        s.activation(o1s[p][:, 0:512], xs[p][:, 0:512], AF.Identity,
                                     bias=nmrp[p], scale=r1p[p]).then_inc(sc, 1)
                        s.activation(o1s[p][:, 512:1024], xs[p][:, 512:1024],
                                     AF.Identity,
                                     bias=nmrp[p], scale=r1p[p]).then_inc(sc, 1)
                    else:
                        s.activation(o1s[p][:, :], xs[p][:, :], AF.Identity,
                                     bias=nmrp[p], scale=r1p[p]).then_inc(sc, 1)

                sdisp = {"o1": s_o1, "sq": s_sq, "tp": s_tp, "ln": s_ln}
                for k, j in _SORDER:
                    sdisp[k](j)


    return nc


_NC_CACHE = {}


def kernel(**inp):
    x = np.ascontiguousarray(np.asarray(inp["L0_attn_x"], dtype=np.float32))
    wq = np.asarray(inp["L0_attn_w_q"], dtype=np.float32)
    bq = np.asarray(inp["L0_attn_b_q"], dtype=np.float32)
    wk = np.asarray(inp["L0_attn_w_k"], dtype=np.float32)
    bk = np.asarray(inp["L0_attn_b_k"], dtype=np.float32)
    wv = np.asarray(inp["L0_attn_w_v"], dtype=np.float32)
    bv = np.asarray(inp["L0_attn_b_v"], dtype=np.float32)
    k0 = np.asarray(inp["L0_attn_k_0"], dtype=np.float32)
    v0 = np.asarray(inp["L0_attn_v_0"], dtype=np.float32)
    W = np.asarray(inp["L0_ffn_W"], dtype=np.float32)
    b = np.asarray(inp["L0_ffn_b"], dtype=np.float32)
    ld = np.asarray(inp["L0_ffn_lora_down"], dtype=np.float32)
    lu = np.asarray(inp["L0_ffn_lora_up"], dtype=np.float32)
    da = np.asarray(inp["L0_ffn_down_aux"], dtype=np.float32)
    ua = np.asarray(inp["L0_ffn_up_aux"], dtype=np.float32)

    wqkT = np.ascontiguousarray(
        np.concatenate([wq[:128].T, wk[:128].T, wq[128:129].T, wk[128:129].T], axis=1))
    lpack = np.zeros((128, 2112), dtype=np.float32)
    lpack[0:100, 0:1024] = da
    lpack[0:50, 1024:2048] = ua.T
    lpack[0:100, 2048:2052] = ld.T
    lpack[0:4, 2052:2102] = lu.T
    browp = np.zeros((128, 1024), dtype=np.float32)
    browp[0] = b
    onesp = np.zeros((128, 2048), dtype=ml_dtypes.bfloat16)
    onesp[0] = 1.0
    bcols = np.zeros((128, 3), dtype=np.float32)
    bcols[:, 0] = bq[:128]
    bcols[:, 1] = bk[:128]
    bcols[0, 2] = bq[128]
    bcols[1, 2] = bk[128]

    common = dict(
        wqkT=wqkT, bcols=bcols,
        wvT=np.ascontiguousarray(wv.T),
        bvb=np.ascontiguousarray(np.tile(bv[None, :], (128, 1))),
        k0m=np.ascontiguousarray(k0[:128, None]),
        k0e=np.ascontiguousarray(k0[128:129, None]),
        v0r=np.ascontiguousarray(v0[None, :]),
        WT=np.ascontiguousarray(W.T),
        browp=browp, onesp=onesp,
        lpack=lpack,
        idf_in=np.eye(128, dtype=np.float32),
    )

    if "nc" not in _NC_CACHE:
        _NC_CACHE["nc"] = build_nc()
    nc = _NC_CACHE["nc"]

    in_maps = []
    for core in range(8):
        m = dict(common)
        m["x"] = np.ascontiguousarray(x[core])
        m["xT"] = np.ascontiguousarray(x[core].T)
        in_maps.append(m)

    res = run_bass_kernel_spmd(nc, in_maps, core_ids=list(range(8)))
    out = np.stack([np.asarray(res.results[core]["out"]) for core in range(8)], axis=0)
    return out.astype(np.float32)


# version 28
# speedup vs baseline: 1.0092x; 1.0092x over previous
import numpy as np
import ml_dtypes
import contextlib
import concourse.bass as bass
import concourse.mybir as mybir
from concourse.bass_utils import run_bass_kernel_spmd

F32 = mybir.dt.float32
BF16 = mybir.dt.bfloat16
AX = mybir.AxisListType.X
OP = mybir.AluOpType
AF = mybir.ActivationFunctionType

S = 2048
D = 1024
NT = 16
ND = 8
NC4 = 4
EPS = 1e-5
VW = 1025

DG_WV = 16 * 12
DG_E2SPLIT = 16 * 13
DSY_PRO = 48
T_QM, T_KM, T_E2 = 1, 2, 3
T_VP0 = 3
T_PSK = 20
T_SA0 = 20
T_PACKT = 37
V_ONES = 1
V_VHONES = 2
V_VE0 = 2
V_SSINK = 19
V_M0 = 19
V_PACK = 36
V_PACKT = 37
SC_QM, SC_KM, SC_E2, SC_PSINK = 1, 2, 3, 4
T0B, SC0B = T_PACKT, SC_PSINK
TB = lambda tt: T0B + tt + 1
SCB = lambda tt: SC0B + tt + 1
T0C, V0C, SC0C = T0B + NT, V_PACKT, SC0B + NT
TC = lambda j: T0C + j + 1
VC_RL = lambda j: V0C + 3 * j + 1
VC_Y = lambda j: (V0C + 3 * j + 2) if j < NT - 1 else V0C + 49
VC_R1 = lambda j: (V0C + 3 * j + 3) if j < NT - 1 else V0C + 50
DX_X = lambda j: 16 * (j // 2 + 1)
DX_WT = lambda c: 128 + 16 * (c // 2 + 1)
DX_OUT = lambda j: 192 + 16 * (j // 2 + 1)
DX_FINAL = 192 + 16 * (NT // 2)
T0D, V0D, SC0D, DG0D = T0C + NT, V0C + 3 * NT, SC0C + 2 * NT, DG_E2SPLIT
T_A, T_B = T0D + 1, T0D + 2
T_CT = lambda c: T0D + 2 + c + 1
V_A, V_B = V0C + 47, V0C + 48
V_WEFF = lambda c: V0D + 2 + c + 1
_TORDER = ([("tp", 0), ("tp", 1), ("A", 0), ("tp", 2), ("B", 0), ("tp", 3)]
           + [x for c in range(ND) for x in [("ct", c), ("tp", c + 4)]]
           + [("tp", 12), ("tp", 13), ("ffn", 0), ("tp", 14), ("ffn", 1), ("tp", 15)]
           + [("ffn", j) for j in range(2, NT)])
_T_TP = [0] * NT
_T_FO = [0] * NT
_T_CT = [0] * ND
_cnt = T0D
T_A = T_B = T_FOH15 = 0
for _k, _j in _TORDER:
    _cnt += 1
    if _k == "tp":
        _T_TP[_j] = _cnt
    elif _k == "ffn":
        if _j == NT - 1:
            T_FOH15 = _cnt
            _cnt += 1
        _T_FO[_j] = _cnt
    elif _k == "ct":
        _T_CT[_j] = _cnt
    elif _k == "A":
        T_A = _cnt
    else:
        T_B = _cnt
T_TP = lambda j: _T_TP[j]
T_FO = lambda j: _T_FO[j]
T_CT = lambda c: _T_CT[c]
_V_Z = [0] * NT
_V_R2 = [0] * NT
_cnt = V0D + 10
for _j in range(NT):
    _cnt += 1; _V_Z[_j] = _cnt
    _cnt += 1; _V_R2[_j] = _cnt
V_Z = lambda j: _V_Z[j]
V_R2 = lambda j: _V_R2[j]
_SORDER = []
for _j in range(NT - 1):
    _SORDER += [("o1", _j), ("sq", _j)]
_SORDER += [("o1", NT - 1), ("tp", 0), ("tp", 1), ("tp", 2), ("tp", 3),
            ("sq", NT - 1)]
_SORDER += [("tp", _j) for _j in range(4, 14)]
for _j in range(NT):
    _SORDER.append(("ln", _j))
    if _j + 14 < NT:
        _SORDER.append(("tp", _j + 14))
_SC_O1 = [0] * NT
_SC_SQ = [0] * NT
_SC_TP = [0] * NT
_SCD_SQ = [0] * NT
_SCD_OUT = [0] * NT
_cnt = SC0C
for _k, _j in _SORDER:
    if _k == "o1":
        _cnt += 1; _SC_O1[_j] = _cnt
    elif _k == "sq":
        _cnt += 1; _SC_SQ[_j] = _cnt
    elif _k == "tp":
        _cnt += 1; _SC_TP[_j] = _cnt
    else:
        _cnt += 1; _SCD_SQ[_j] = _cnt
        _cnt += 1; _SCD_OUT[_j] = _cnt
SCC_O1 = lambda j: _SC_O1[j]
SCC_SQ = lambda j: _SC_SQ[j]
SC_TP = lambda j: _SC_TP[j]
SCD_SQ = lambda j: _SCD_SQ[j]
SCD_OUT = lambda j: _SCD_OUT[j]
DLP_PACKT = 32
DGP_ALL = 64


def build_nc(debug=False):
    nc = bass.Bass("TRN2", target_bir_lowering=False)

    x_d = nc.dram_tensor("x", [S, D], F32, kind="ExternalInput")
    xT_d = nc.dram_tensor("xT", [D, S], F32, kind="ExternalInput")
    wqkT_d = nc.dram_tensor("wqkT", [D, 258], F32, kind="ExternalInput")
    bcols_d = nc.dram_tensor("bcols", [128, 3], F32, kind="ExternalInput")
    wvT_d = nc.dram_tensor("wvT", [D, D], F32, kind="ExternalInput")
    bvb_d = nc.dram_tensor("bvb", [128, D], F32, kind="ExternalInput")
    k0m_d = nc.dram_tensor("k0m", [128, 1], F32, kind="ExternalInput")
    k0e_d = nc.dram_tensor("k0e", [1, 1], F32, kind="ExternalInput")
    v0r_d = nc.dram_tensor("v0r", [1, D], F32, kind="ExternalInput")
    WT_d = nc.dram_tensor("WT", [D, D], F32, kind="ExternalInput")
    lpack_d = nc.dram_tensor("lpack", [128, 2112], F32, kind="ExternalInput")
    onesp_d = nc.dram_tensor("onesp", [128, S], BF16, kind="ExternalInput")
    browp_d = nc.dram_tensor("browp", [128, D], F32, kind="ExternalInput")
    idf_d = nc.dram_tensor("idf_in", [128, 128], F32, kind="ExternalInput")
    out_d = nc.dram_tensor("out", [S, D], F32, kind="ExternalOutput")
    if debug:
        dbg = {nm: nc.dram_tensor("dbg_" + nm, shp, F32, kind="ExternalOutput")
               for nm, shp in [("mcols", [128, NT]), ("ssink", [128, NT]),
                                ("qe2", [2, S]), ("ke2", [2, S]), ("ptsink", [1, S]),
                                ("qmt", [128, S]), ("kmt", [128, S]),
                                ("pt0", [128, S]), ("x1f", [128, NT * D]),
                                ("packtf", [32, 128]), ("vh0", [128, VW]),
                                ("weff0", [128, D]), ("x1t0", [128, S])]}

    es = contextlib.ExitStack()
    sb = lambda name, shape, dt: es.enter_context(nc.sbuf_tensor(name, shape, dt))

    with es:
        QmT = sb("QmT", [128, S], BF16)
        KmT = sb("KmT", [128, S], BF16)
        Qe2 = sb("Qe2", [128, S], BF16)
        Ke2 = sb("Ke2", [128, S], BF16)
        Vh = sb("Vh", [128, NT * VW], BF16)
        Vs = sb("Vs", [128, VW], BF16)
        PTsink = sb("PTsink", [128, S], BF16)
        k0mb = sb("k0mb", [128, 1], BF16)
        k0eb = sb("k0eb", [1, 1], BF16)
        ssink = sb("ssink", [128, NT], F32)
        mcols = sb("mcols", [128, NT], F32)
        negmb = sb("negmb", [128, NT], BF16)
        psd = sb("psd", [128, NT], F32)
        packf = sb("packf", [128, 32], F32)
        packTf = sb("packTf", [32, 128], BF16)
        idf = sb("idf", [128, 128], F32)
        X1f = sb("X1f", [128, NT * D], F32)
        xs = [sb(f"xs{i}", [128, D], F32) for i in range(2)]
        o1s = [sb(f"o1s{i}", [128, D], F32) for i in range(2)]
        st12 = [sb(f"st12{i}", [128, 12], F32) for i in range(2)]
        smalls = sb("smalls", [128, 16], F32)
        tmpm = smalls[:, 0:1]
        sigp = [smalls[:, 1 + i:2 + i] for i in range(2)]
        sigs = [smalls[:, 3 + i:4 + i] for i in range(2)]
        r1p = [smalls[:, 5 + i:6 + i] for i in range(2)]
        nmrp = [smalls[:, 7 + i:8 + i] for i in range(2)]
        rls = [smalls[:, 9 + i:10 + i] for i in range(2)]
        epsc = smalls[:, 11:12]
        modd = sb("modd", [128, NT], F32)
        mvp2 = sb("mvp2", [128, 4], F32)
        mvp = [mvp2[:, 2 * i:2 * i + 2] for i in range(2)]

        dg = es.enter_context(nc.semaphore("dg"))
        dsy = es.enter_context(nc.semaphore("dsy"))
        t = es.enter_context(nc.semaphore("t"))
        v = es.enter_context(nc.semaphore("v"))
        sc = es.enter_context(nc.semaphore("sc"))
        dgc = [es.enter_context(nc.semaphore(f"dgc{i}")) for i in range(ND)]
        dx = [es.enter_context(nc.semaphore(f"dx{i}")) for i in range(2)]
        dlp = es.enter_context(nc.semaphore("dlp"))
        dgp = es.enter_context(nc.semaphore("dgp"))

        AR = sb("AR", [128, 32768], BF16)
        wqkb = AR[:, 0:ND * 258]
        wvb = AR[:, 2064:2064 + ND * D]
        XT = AR[:, 10256:10256 + ND * S]
        PT = AR
        X1T = AR
        bvbs = X1f[:, 0:D]
        bcols = X1f[:, D:D + 3]
        ys = [sb(f"ys{i}", [128, D], F32) for i in range(2)]
        pq = es.enter_context(nc.psum_tensor("pq", [128, 2048], F32))
        pv = [es.enter_context(nc.psum_tensor(f"pv{i}", [128, 1024], F32)) for i in range(2)]
        PO = [pq[:, 0:1024], pq[:, 1024:2048]]
        PL = [pv[0][:, 0:512], pv[0][:, 512:1024]]
        TP0 = pv[1]
        FO = PO
        WeffT = Vh[:, 0:8192]
        Ab = Vh[0:4, 13376:14400]
        Bb = Vh[:, 14400:15424]
        WTs = xs
        dab = QmT[0:100, 0:1024]
        uab = QmT[:, 1024:2048]
        ldb = KmT[0:100, 0:4]
        lub = KmT[0:4, 4:54]
        bbr = KmT[:, 64:1088]
        onesS = Qe2[:, 0:2048]

        with nc.Block() as blk:
            @blk.gpsimd
            def _(g):
                for c in range(ND):
                    g.dma_start(wqkb[:, c * 258:(c + 1) * 258],
                                wqkT_d[c * 128:(c + 1) * 128, :]).then_inc(dgc[c], 16)
                    g.dma_start(XT[:, c * S:(c + 1) * S],
                                xT_d[c * 128:(c + 1) * 128, :]).then_inc(dgc[c], 16)
                for c in range(ND):
                    g.dma_start(wvb[:, c * D:(c + 1) * D],
                                wvT_d[c * 128:(c + 1) * 128, :]).then_inc(dg, 16)
                g.dma_start(k0mb[:, :], k0m_d[:, :]).then_inc(dg, 16)
                g.dma_start(k0eb[:, :], k0e_d[:, :]).then_inc(dg, 16)
                g.wait_ge(v, V_ONES)
                g.dma_start(Vs[0:1, 0:D], v0r_d[:, :]).then_inc(dg, 16)
                g.dma_start(Ke2[1:2, :], PTsink[0:1, :]).then_inc(dg, 16)
                g.wait_ge(sc, SC_E2)
                g.wait_ge(dg, DG_WV)
                g.dma_start(Ke2[0:1, :], Qe2[1:2, :]).then_inc(dg, 16)
                g.wait_ge(t, TB(NT - 1))
                g.dma_start(QmT[:, :], lpack_d[:, 0:2048]).then_inc(dgp, 16)
                g.dma_start(KmT[:, 0:64], lpack_d[:, 2048:2112]).then_inc(dgp, 16)
                g.dma_start(bbr, browp_d[:, :]).then_inc(dgp, 16)
                g.dma_start(onesS, onesp_d[:, :]).then_inc(dgp, 16)

            @blk.sync
            def _(sy):
                sy.dma_start(bvbs[:, :], bvb_d[:, :]).then_inc(dsy, 16)
                sy.dma_start(bcols[:, :], bcols_d[:, :]).then_inc(dsy, 16)
                sy.dma_start(idf[:, :], idf_d[:, :]).then_inc(dsy, 16)
                sy.wait_ge(v, V_PACKT)
                sy.dma_start(Qe2[1:2, :], packTf[0:16, :]).then_inc(dlp, 16)
                sy.dma_start(PTsink[0:1, :], packTf[16:32, :]).then_inc(dlp, 16)

                for j in range(NT):
                    if j >= 2:
                        sy.wait_ge(v, VC_Y(j - 2))
                    sy.dma_start(xs[j % 2][:, :],
                                 x_d[j * 128:(j + 1) * 128, :]).then_inc(dx[j % 2], 16)
                for c in range(ND):
                    if c == 0:
                        sy.wait_ge(v, VC_Y(NT - 2))
                    elif c == 1:
                        sy.wait_ge(v, VC_Y(NT - 1))
                    else:
                        sy.wait_ge(v, V_WEFF(c - 2))
                    sy.dma_start(WTs[c % 2][:, :],
                                 WT_d[c * 128:(c + 1) * 128, :]).then_inc(dx[c % 2], 16)
                for j in range(NT):
                    if j == NT - 1:
                        sy.wait_ge(sc, SCD_OUT(j))
                        sy.dma_start(out_d[j * 128:(j + 1) * 128, 0:512],
                                     o1s[j % 2][:, 0:512]).then_inc(dx[j % 2], 16)
                        sy.wait_ge(sc, SCD_OUT(j) + 1)
                        sy.dma_start(out_d[j * 128:(j + 1) * 128, 512:1024],
                                     o1s[j % 2][:, 512:1024]).then_inc(dx[j % 2], 16)
                    else:
                        sy.wait_ge(sc, SCD_OUT(j))
                        sy.dma_start(out_d[j * 128:(j + 1) * 128, :],
                                     o1s[j % 2][:, :]).then_inc(dx[j % 2], 16)
                sy.wait_ge(dx[0], DX_FINAL)
                sy.wait_ge(dx[1], DX_FINAL + 16)

            @blk.tensor
            def _(te):
                for c in range(ND):
                    te.wait_ge(dgc[c], 32)
                    for n in range(NC4):
                        n0, n1 = n * 512, (n + 1) * 512
                        te.matmul(pq[:, n0:n1],
                                  wqkb[:, c * 258: c * 258 + 128],
                                  XT[:, c * S + n0: c * S + n1],
                                  start=(c == 0), stop=(c == ND - 1))
                    for n in range(NC4):
                        n0, n1 = n * 512, (n + 1) * 512
                        outp = (pv[n // 2])[:, (n % 2) * 512:(n % 2 + 1) * 512]
                        te.matmul(outp,
                                  wqkb[:, c * 258 + 128: c * 258 + 256],
                                  XT[:, c * S + n0: c * S + n1],
                                  start=(c == 0), stop=(c == ND - 1))
                te.drain(fusable=True).then_inc(t, 1)
                te.drain(fusable=True).then_inc(t, 1)
                te.wait_ge(sc, SC_QM)
                for n in range(NC4):
                    n0, n1 = n * 512, (n + 1) * 512
                    for c in range(ND):
                        ins = te.matmul(pq[0:2, n0:n1],
                                        wqkb[:, c * 258 + 256: c * 258 + 258],
                                        XT[:, c * S + n0: c * S + n1],
                                        start=(c == 0), stop=(c == ND - 1))
                te.drain(fusable=True).then_inc(t, 1)
                for tt in range(NT):
                    if tt < 2:
                        te.wait_ge(sc, SC_KM)
                    else:
                        te.wait_ge(v, V_VE0 + (tt - 2) + 1)
                    for n in range(2):
                        n0, n1 = n * 512, (n + 1) * 512
                        for c in range(ND):
                            if tt == 0 and n == 0 and c == 0:
                                te.wait_ge(dg, DG_WV)
                            ins = te.matmul(pv[tt % 2][:, n0:n1],
                                            XT[:, c * S + tt * 128: c * S + (tt + 1) * 128],
                                            wvb[:, c * D + n0: c * D + n1],
                                            start=(c == 0), stop=(c == ND - 1))
                    te.drain(fusable=True).then_inc(t, 1)
                te.wait_ge(sc, SC_E2)
                te.wait_ge(dg, DG_E2SPLIT)
                for j in range(NT):
                    te.matmul(pq[:, j:j + 1], QmT[:, j * 128:(j + 1) * 128],
                              k0mb[:, :], start=True, stop=False)
                    ins = te.matmul(pq[:, j:j + 1], Qe2[0:1, j * 128:(j + 1) * 128],
                                    k0eb[:, :], start=False, stop=True)
                te.drain(fusable=True).then_inc(t, 1)
                for j in range(NT):
                    if j == 0:
                        te.wait_ge(v, V_SSINK)
                    elif j == 1:
                        te.wait_ge(v, V_VE0 + NT)
                    else:
                        te.wait_ge(v, V_M0 + (j - 2) + 1)
                    for n in range(NC4):
                        n0, n1 = n * 512, (n + 1) * 512
                        if j % 2 == 0:
                            outp = pq[:, n0:n1]
                        else:
                            outp = (pv[n // 2])[:, (n % 2) * 512:(n % 2 + 1) * 512]
                        te.matmul(outp, QmT[:, j * 128:(j + 1) * 128],
                                  KmT[:, n0:n1], start=True, stop=False)
                        te.matmul(outp, Qe2[:, j * 128:(j + 1) * 128],
                                  Ke2[:, n0:n1], start=False, stop=True)
                    te.drain(fusable=True).then_inc(t, 1)
                te.wait_ge(sc, SC_PSINK)
                te.wait_ge(dsy, DSY_PRO)
                te.matmul(pv[0][0:32, 0:128], packf[:, :], idf[:, :],
                          is_transpose=True, start=True, stop=True)
                te.drain(fusable=True).then_inc(t, 1)

                def sb5(tt, n):
                    if tt % 2 == 0:
                        return pq[:, n * 512:(n + 1) * 512]
                    return (pv[n // 2])[:, (n % 2) * 512:(n % 2 + 1) * 512]
                te.wait_ge(v, V_M0 + 15)
                for n in range(NC4):
                    te.matmul(sb5(0, n), KmT[:, 0:128],
                              QmT[:, n * 512:(n + 1) * 512], start=True, stop=False)
                te.wait_ge(v, V_PACKT)
                for n in range(NC4):
                    te.matmul(sb5(1, n), KmT[:, 128:256],
                              QmT[:, n * 512:(n + 1) * 512], start=True, stop=False)
                te.wait_ge(dlp, DLP_PACKT)
                for tt in range(NT):
                    if tt >= 2:
                        te.wait_ge(sc, SCB(tt - 2))
                    for n in range(NC4):
                        n0, n1 = n * 512, (n + 1) * 512
                        if tt >= 2:
                            te.matmul(sb5(tt, n),
                                      KmT[:, tt * 128:(tt + 1) * 128],
                                      QmT[:, n0:n1], start=True, stop=False)
                        te.matmul(sb5(tt, n),
                                  Ke2[:, tt * 128:(tt + 1) * 128],
                                  Qe2[:, n0:n1], start=False, stop=True)
                    te.drain(fusable=True).then_inc(t, 1)

                te.wait_ge(dlp, DLP_PACKT)
                te.wait_ge(v, V_PACKT)
                te.wait_ge(sc, SC0C)
                for j in range(NT):
                    if j >= 2:
                        te.wait_ge(sc, SCC_O1(j - 2))
                        te.wait_ge(v, VC_RL(j - 2))
                    for c in range(NT + 1):
                        first, last = (c == 0), (c == NT)
                        if c < NT:
                            lhs = PT[:, c * S + j * 128: c * S + (j + 1) * 128]
                            r1_ = Vh[:, c * VW: c * VW + 512]
                            r2_ = Vh[:, c * VW + 512: c * VW + 1024]
                            r3_ = Vh[:, c * VW + D: c * VW + VW]
                        else:
                            lhs = PTsink[:, j * 128:(j + 1) * 128]
                            r1_ = Vs[:, 0:512]
                            r2_ = Vs[:, 512:1024]
                            r3_ = Vs[:, D:VW]
                        te.matmul(PL[j % 2][:, 0:1], lhs, r3_, start=first, stop=last)
                        te.matmul(PO[j % 2][:, 0:512], lhs, r1_, start=first, stop=last)
                        te.matmul(PO[j % 2][:, 512:1024], lhs, r2_, start=first, stop=last)
                    te.drain(fusable=True).then_inc(t, 1)

                def emit_A():
                    te.wait_ge(dgp, DGP_ALL)
                    te.wait_ge(sc, SCC_O1(NT - 2))
                    for n in range(2):
                        te.matmul(FO[0][0:4, n * 512:(n + 1) * 512], ldb,
                                  dab[:, n * 512:(n + 1) * 512], start=True, stop=True)
                    te.drain(fusable=True).then_inc(t, 1)

                def emit_B():
                    te.wait_ge(v, V_A)
                    for n in range(2):
                        te.matmul(FO[0][0:50, n * 512:(n + 1) * 512], lub,
                                  Ab[:, n * 512:(n + 1) * 512], start=True, stop=True)
                    te.drain(fusable=True).then_inc(t, 1)

                def emit_ct(c):
                    if c == 1:
                        te.wait_ge(sc, SCC_O1(NT - 1))
                    te.wait_ge(v, V_B if c < 2 else V_WEFF(c - 2))
                    for n in range(2):
                        te.matmul(FO[c % 2][:, n * 512:(n + 1) * 512],
                                  Bb[:, c * 128:(c + 1) * 128],
                                  uab[:, n * 512:(n + 1) * 512], start=True, stop=True)
                    te.drain(fusable=True).then_inc(t, 1)

                def emit_tp(j):
                    te.wait_ge(v, VC_R1(j))
                    if j == 1:
                        te.wait_ge(v, VC_RL(NT - 1))
                    if j >= 2:
                        te.wait_ge(sc, SC_TP(j - 2))
                    for b in range(ND):
                        if j % 2 == 0:
                            outp = TP0[:, b * 128:(b + 1) * 128]
                        else:
                            outp = PL[b // 4][:, (b % 4) * 128:(b % 4 + 1) * 128]
                        te.matmul(outp,
                                  X1f[:, j * D + b * 128: j * D + (b + 1) * 128],
                                  idf[:, :], is_transpose=True,
                                  start=(b % 4 == 0), stop=(b % 4 == 3))
                    te.drain(fusable=True).then_inc(t, 1)

                def emit_ffn(j):
                    if j == 0:
                        te.wait_ge(dgp, DGP_ALL)
                    te.wait_ge(sc, SC_TP(j))
                    if j < 2:
                        te.wait_ge(v, V_WEFF(ND - 1))
                    else:
                        te.wait_ge(v, V_Z(j - 2))
                    for n in range(2):
                        n0, n1 = n * 512, (n + 1) * 512
                        for c in range(ND):
                            te.matmul(FO[j % 2][:, n0:n1],
                                      X1T[:, c * S + j * 128: c * S + (j + 1) * 128],
                                      WeffT[:, c * D + n0: c * D + n1],
                                      start=(c == 0), stop=False)
                        te.matmul(FO[j % 2][:, n0:n1],
                                  onesS[:, j * 128:(j + 1) * 128],
                                  bbr[:, n0:n1], start=False, stop=True)
                        if j == NT - 1 and n == 0:
                            te.drain(fusable=True).then_inc(t, 1)
                    te.drain(fusable=True).then_inc(t, 1)

                emitters = {"tp": emit_tp, "ffn": emit_ffn, "ct": emit_ct,
                            "A": lambda j: emit_A(), "B": lambda j: emit_B()}
                for k, j in _TORDER:
                    emitters[k](j)

            def emit_z(ve, j):
                p = j % 2
                if j == NT - 1:
                    ve.wait_ge(t, T_FOH15)
                    ve.tensor_tensor(xs[p][:, 0:512], X1f[:, j * D: j * D + 512],
                                     FO[p][:, 0:512], OP.add)
                    ve.drain()
                    ve.bn_stats(st12[p][:, 0:6], xs[p][:, 0:512])
                    ve.wait_ge(t, T_FO(j))
                    ve.tensor_tensor(xs[p][:, 512:1024],
                                     X1f[:, j * D + 512:(j + 1) * D],
                                     FO[p][:, 512:1024], OP.add)
                    ve.drain()
                    ve.bn_stats(st12[p][:, 6:12], xs[p][:, 512:1024])
                    ve.drain()
                else:
                    ve.wait_ge(t, T_FO(j))
                    ve.tensor_tensor(xs[p][:, 0:512], X1f[:, j * D: j * D + 512],
                                     FO[p][:, 0:512], OP.add)
                    ve.tensor_tensor(xs[p][:, 512:1024],
                                     X1f[:, j * D + 512:(j + 1) * D],
                                     FO[p][:, 512:1024], OP.add)
                    ve.drain()
                    ve.bn_stats(st12[p][:, 0:6], xs[p][:, 0:512])
                    ve.bn_stats(st12[p][:, 6:12], xs[p][:, 512:1024])
                    ve.drain()
                ve.bn_aggr(mvp[p], st12[p][:, :])
                ve.drain().then_inc(v, 1)
                ve.wait_ge(sc, SCD_SQ(j))
                ve.reciprocal(r1p[p], sigs[p])
                ve.drain()
                ve.tensor_scalar(nmrp[p], mvp[p][:, 0:1], r1p[p],
                                 -1.0, OP.mult, OP.mult)
                ve.drain().then_inc(v, 1)

            @blk.vector
            def _(ve):
                ve.memset(epsc, EPS)
                ve.memset(Qe2[:, :], 0.0)
                ve.memset(Ke2[:, :], 0.0)
                ve.memset(PTsink[:, :], 0.0)
                ve.memset(Vs[:, :], 0.0)
                ve.drain()
                ve.memset(PTsink[0:1, :], 1.0)
                ve.memset(Vs[0:1, D:VW], 1.0).then_inc(v, 1)
                ve.memset(bass.AP(Vh, D, [[NT * VW, 128], [VW, NT]]), 1.0).then_inc(v, 1)
                ve.wait_ge(dsy, DSY_PRO)
                for tt in range(NT):
                    ve.wait_ge(t, T_VP0 + tt + 1)
                    ve.tensor_tensor(Vh[:, tt * VW: tt * VW + D], pv[tt % 2][:, :],
                                     bvbs[:, :], OP.add).then_inc(v, 1)
                ve.wait_ge(t, T_PSK)
                ve.tensor_copy(ssink[:, :], pq[:, 0:NT]).then_inc(v, 1)
                for j in range(NT):
                    ve.wait_ge(t, T_SA0 + j + 1)
                    if j % 2 == 0:
                        ve.reduce_max(mcols[:, j:j + 1], pq[:, :], axis=AX).then_inc(v, 1)
                    else:
                        jo = j // 2
                        ve.reduce_max(modd[:, 2 * jo:2 * jo + 1], pv[0][:, :], axis=AX)
                        ve.reduce_max(modd[:, 2 * jo + 1:2 * jo + 2], pv[1][:, :],
                                      axis=AX).then_inc(v, 1)
                ve.drain()
                ve.tensor_tensor(bass.AP(mcols, 1, [[NT, 128], [2, 8]]),
                                 bass.AP(modd, 0, [[NT, 128], [2, 8]]),
                                 bass.AP(modd, 1, [[NT, 128], [2, 8]]), OP.max)
                ve.drain()
                ve.tensor_tensor(mcols[:, :], mcols[:, :], ssink[:, :], OP.max)
                ve.drain()
                ve.tensor_scalar_mul(negmb[:, :], mcols[:, :], -1.0)
                ve.drain()
                ve.tensor_tensor(psd[:, :], ssink[:, :], negmb[:, :], OP.add)
                ve.tensor_copy(packf[:, 0:16], negmb[:, :])
                ve.drain().then_inc(v, 1)
                ve.wait_ge(t, T_PACKT)
                ve.tensor_copy(packTf[:, :], pv[0][0:32, 0:128]).then_inc(v, 1)

                def p6a_head(j):
                    p = j % 2
                    ve.wait_ge(t, TC(j))
                    ve.reciprocal(rls[p], PL[p][:, 0:1])
                    ve.drain().then_inc(v, 1)

                def p6a_ln(j):
                    p = j % 2
                    ve.wait_ge(sc, SCC_O1(j))
                    ve.wait_ge(dx[j % 2], DX_X(j))
                    ve.scalar_tensor_tensor(ys[p][:, 0:512], xs[p][:, 0:512], 2.0,
                                            o1s[p][:, 0:512], OP.mult, OP.add)
                    ve.scalar_tensor_tensor(ys[p][:, 512:1024], xs[p][:, 512:1024], 2.0,
                                            o1s[p][:, 512:1024], OP.mult, OP.add)
                    ve.drain()
                    ve.bn_stats(st12[p][:, 0:6], ys[p][:, 0:512])
                    ve.bn_stats(st12[p][:, 6:12], ys[p][:, 512:1024])
                    ve.drain()
                    ve.bn_aggr(mvp[p], st12[p][:, :])
                    ve.drain().then_inc(v, 1)
                    ve.wait_ge(sc, SCC_SQ(j))
                    ve.reciprocal(r1p[p], sigs[p])
                    ve.drain()
                    ve.tensor_scalar(nmrp[p], mvp[p][:, 0:1], r1p[p],
                                     -1.0, OP.mult, OP.mult)
                    ve.drain()
                    ve.tensor_scalar(X1f[:, j * D:(j + 1) * D], ys[p][:, :],
                                     r1p[p], nmrp[p], OP.mult, OP.add)
                    ve.drain().then_inc(v, 1)

                for j in range(NT - 1):
                    p6a_head(j)
                    p6a_ln(j)
                p6a_head(NT - 1)
                ve.memset(Bb, 0.0)
                ve.wait_ge(t, T_A)
                ve.tensor_copy(Ab, FO[0][0:4, :])
                ve.drain().then_inc(v, 1)
                ve.wait_ge(t, T_B)
                ve.tensor_copy(Bb[0:50, :], FO[0][0:50, :])
                ve.drain().then_inc(v, 1)
                p6a_ln(NT - 1)
                for c in range(ND):
                    ve.wait_ge(t, T_CT(c))
                    ve.wait_ge(dx[c % 2], DX_WT(c))
                    ve.tensor_tensor(WeffT[:, c * D:(c + 1) * D], FO[c % 2][:, :],
                                     WTs[c % 2][:, :], OP.add)
                    ve.drain().then_inc(v, 1)
                for j in range(NT):
                    emit_z(ve, j)

            @blk.scalar
            def _(s):
                s.wait_ge(v, V_ONES)
                s.activation(o1s[0][:, 0:1], epsc, AF.Identity,
                             bias=epsc, scale=1.0)
                s.activation(o1s[1][:, 0:1], epsc, AF.Copy)
                s.activation(ys[0][:, 0:1], epsc, AF.Exp)
                s.activation(ys[1][:, 0:1], epsc, AF.Sqrt, bias=epsc, scale=1.0)
                s.wait_ge(t, T_QM)
                s.wait_ge(v, V_ONES)
                s.wait_ge(dsy, DSY_PRO)
                s.activation(QmT[:, :], pq[:, :], AF.Identity,
                             bias=bcols[:, 0:1], scale=1.0).then_inc(sc, 1)
                s.wait_ge(t, T_KM)
                s.activation(KmT[:, 0:1024], pv[0][:, :], AF.Identity,
                             bias=bcols[:, 1:2], scale=1.0)
                s.activation(KmT[:, 1024:2048], pv[1][:, :], AF.Identity,
                             bias=bcols[:, 1:2], scale=1.0).then_inc(sc, 1)
                s.wait_ge(t, T_E2)
                s.activation(Qe2[0:2, :], pq[0:2, :], AF.Identity,
                             bias=bcols[0:2, 2:3], scale=1.0).then_inc(sc, 1)
                s.wait_ge(v, V_PACK)
                s.activation(packf[:, 16:32], psd[:, :], AF.Exp).then_inc(sc, 1)

                for tt in range(NT):
                    s.wait_ge(t, TB(tt))
                    if tt % 2 == 0:
                        s.activation(PT[:, tt * S:(tt + 1) * S], pq[:, :],
                                     AF.Exp).then_inc(sc, 1)
                    else:
                        s.activation(PT[:, tt * S: tt * S + 1024], pv[0][:, :], AF.Exp)
                        s.activation(PT[:, tt * S + 1024:(tt + 1) * S], pv[1][:, :],
                                     AF.Exp).then_inc(sc, 1)

                def s_o1(j):
                    p = j % 2
                    s.wait_ge(v, VC_RL(j))
                    s.activation(o1s[p][:, :], PO[p][:, :], AF.Identity,
                                 scale=rls[p]).then_inc(sc, 1)
                def s_sq(j):
                    p = j % 2
                    s.wait_ge(v, VC_Y(j))
                    s.activation(sigs[p], mvp[p][:, 1:2], AF.Sqrt,
                                 bias=epsc, scale=1.0).then_inc(sc, 1)
                def s_tp(j):
                    s.wait_ge(t, T_TP(j))
                    if j % 2 == 0:
                        s.activation(
                            bass.AP(PT, j * 128, [[NT * S, 128], [S, ND], [1, 128]]),
                            TP0[:, :], AF.Copy).then_inc(sc, 1)
                    else:
                        s.activation(
                            bass.AP(PT, j * 128, [[NT * S, 128], [S, 4], [1, 128]]),
                            PL[0][:, :], AF.Copy)
                        s.activation(
                            bass.AP(PT, 4 * S + j * 128,
                                    [[NT * S, 128], [S, 4], [1, 128]]),
                            PL[1][:, :], AF.Copy).then_inc(sc, 1)

                def s_ln(j):
                    p = j % 2
                    s.wait_ge(v, V_Z(j))
                    s.activation(sigs[p], mvp[p][:, 1:2], AF.Sqrt,
                                 bias=epsc, scale=1.0).then_inc(sc, 1)
                    s.wait_ge(v, V_R2(j))
                    if j >= 2:
                        s.wait_ge(dx[j % 2], DX_OUT(j - 2))
                    if j == NT - 1:
                        s.activation(o1s[p][:, 0:512], xs[p][:, 0:512], AF.Identity,
                                     bias=nmrp[p], scale=r1p[p]).then_inc(sc, 1)
                        s.activation(o1s[p][:, 512:1024], xs[p][:, 512:1024],
                                     AF.Identity,
                                     bias=nmrp[p], scale=r1p[p]).then_inc(sc, 1)
                    else:
                        s.activation(o1s[p][:, :], xs[p][:, :], AF.Identity,
                                     bias=nmrp[p], scale=r1p[p]).then_inc(sc, 1)

                sdisp = {"o1": s_o1, "sq": s_sq, "tp": s_tp, "ln": s_ln}
                for k, j in _SORDER:
                    sdisp[k](j)


    return nc


_NC_CACHE = {}


def kernel(**inp):
    x = np.ascontiguousarray(np.asarray(inp["L0_attn_x"], dtype=np.float32))
    wq = np.asarray(inp["L0_attn_w_q"], dtype=np.float32)
    bq = np.asarray(inp["L0_attn_b_q"], dtype=np.float32)
    wk = np.asarray(inp["L0_attn_w_k"], dtype=np.float32)
    bk = np.asarray(inp["L0_attn_b_k"], dtype=np.float32)
    wv = np.asarray(inp["L0_attn_w_v"], dtype=np.float32)
    bv = np.asarray(inp["L0_attn_b_v"], dtype=np.float32)
    k0 = np.asarray(inp["L0_attn_k_0"], dtype=np.float32)
    v0 = np.asarray(inp["L0_attn_v_0"], dtype=np.float32)
    W = np.asarray(inp["L0_ffn_W"], dtype=np.float32)
    b = np.asarray(inp["L0_ffn_b"], dtype=np.float32)
    ld = np.asarray(inp["L0_ffn_lora_down"], dtype=np.float32)
    lu = np.asarray(inp["L0_ffn_lora_up"], dtype=np.float32)
    da = np.asarray(inp["L0_ffn_down_aux"], dtype=np.float32)
    ua = np.asarray(inp["L0_ffn_up_aux"], dtype=np.float32)

    wqkT = np.ascontiguousarray(
        np.concatenate([wq[:128].T, wk[:128].T, wq[128:129].T, wk[128:129].T], axis=1))
    lpack = np.zeros((128, 2112), dtype=np.float32)
    lpack[0:100, 0:1024] = da
    lpack[0:50, 1024:2048] = ua.T
    lpack[0:100, 2048:2052] = ld.T
    lpack[0:4, 2052:2102] = lu.T
    browp = np.zeros((128, 1024), dtype=np.float32)
    browp[0] = b
    onesp = np.zeros((128, 2048), dtype=ml_dtypes.bfloat16)
    onesp[0] = 1.0
    bcols = np.zeros((128, 3), dtype=np.float32)
    bcols[:, 0] = bq[:128]
    bcols[:, 1] = bk[:128]
    bcols[0, 2] = bq[128]
    bcols[1, 2] = bk[128]

    common = dict(
        wqkT=wqkT, bcols=bcols,
        wvT=np.ascontiguousarray(wv.T),
        bvb=np.ascontiguousarray(np.tile(bv[None, :], (128, 1))),
        k0m=np.ascontiguousarray(k0[:128, None]),
        k0e=np.ascontiguousarray(k0[128:129, None]),
        v0r=np.ascontiguousarray(v0[None, :]),
        WT=np.ascontiguousarray(W.T),
        browp=browp, onesp=onesp,
        lpack=lpack,
        idf_in=np.eye(128, dtype=np.float32),
    )

    if "nc" not in _NC_CACHE:
        _NC_CACHE["nc"] = build_nc()
    nc = _NC_CACHE["nc"]

    in_maps = []
    for core in range(8):
        m = dict(common)
        m["x"] = np.ascontiguousarray(x[core])
        m["xT"] = np.ascontiguousarray(x[core].T)
        in_maps.append(m)

    res = run_bass_kernel_spmd(nc, in_maps, core_ids=list(range(8)))
    out = np.stack([np.asarray(res.results[core]["out"]) for core in range(8)], axis=0)
    return out.astype(np.float32)


# version 29
# speedup vs baseline: 1.0278x; 1.0185x over previous
import numpy as np
import ml_dtypes
import contextlib
import concourse.bass as bass
import concourse.mybir as mybir
from concourse.bass_utils import run_bass_kernel_spmd

F32 = mybir.dt.float32
BF16 = mybir.dt.bfloat16
AX = mybir.AxisListType.X
OP = mybir.AluOpType
AF = mybir.ActivationFunctionType

S = 2048
D = 1024
NT = 16
ND = 8
NC4 = 4
EPS = 1e-5
VW = 1025

DG_WV = 16 * 12
DG_E2SPLIT = 16 * 13
DSY_PRO = 48
T_QM, T_KM, T_E2 = 1, 2, 3
T_VP0 = 3
T_PSK = 20
T_SA0 = 20
T_PACKT = 37
V_ONES = 1
V_VHONES = 2
V_VE0 = 2
V_SSINK = 19
V_M0 = 19
V_PACK = 36
V_PACKT = 37
SC_QM, SC_KM, SC_E2, SC_PSINK = 1, 2, 3, 4
T0B, SC0B = T_PACKT, SC_PSINK
TB = lambda tt: T0B + tt + 1
SCB = lambda tt: SC0B + tt + 1 if tt <= 13 else (SC0B + 17 if tt == 14 else SC0B + 18)
SCB14A, SCB15A = SC0B + 15, SC0B + 16
T0C, V0C, SC0C = T0B + NT, V_PACKT, SC0B + NT + 2
TC = lambda j: T0C + j + 1
VC_RL = lambda j: V0C + 3 * j + 1
VC_Y = lambda j: (V0C + 3 * j + 2) if j < NT - 1 else V0C + 49
VC_R1 = lambda j: (V0C + 3 * j + 3) if j < NT - 1 else V0C + 50
DX_X = lambda j: 16 * (j // 2 + 1)
DX_WT = lambda c: 128 + 16 * (c // 2 + 1)
DX_OUT = lambda j: 192 + 16 * (j // 2 + 1)
DX_FINAL = 192 + 16 * (NT // 2)
T0D, V0D, SC0D, DG0D = T0C + NT, V0C + 3 * NT, SC0C + 2 * NT, DG_E2SPLIT
T_A, T_B = T0D + 1, T0D + 2
T_CT = lambda c: T0D + 2 + c + 1
V_A, V_B = V0C + 47, V0C + 48
V_WEFF = lambda c: V0D + 2 + c + 1
_TORDER = ([("tp", 0), ("tp", 1), ("A", 0), ("tp", 2), ("B", 0), ("tp", 3)]
           + [x for c in range(ND) for x in [("ct", c), ("tp", c + 4)]]
           + [("tp", 12), ("tp", 13), ("ffn", 0), ("tp", 14), ("ffn", 1), ("tp", 15)]
           + [("ffn", j) for j in range(2, NT)])
_T_TP = [0] * NT
_T_FO = [0] * NT
_T_CT = [0] * ND
_cnt = T0D
T_A = T_B = T_FOH15 = 0
for _k, _j in _TORDER:
    _cnt += 1
    if _k == "tp":
        _T_TP[_j] = _cnt
    elif _k == "ffn":
        if _j == NT - 1:
            T_FOH15 = _cnt
            _cnt += 1
        _T_FO[_j] = _cnt
    elif _k == "ct":
        _T_CT[_j] = _cnt
    elif _k == "A":
        T_A = _cnt
    else:
        T_B = _cnt
T_TP = lambda j: _T_TP[j]
T_FO = lambda j: _T_FO[j]
T_CT = lambda c: _T_CT[c]
_V_Z = [0] * NT
_V_R2 = [0] * NT
_cnt = V0D + 10
for _j in range(NT):
    _cnt += 1; _V_Z[_j] = _cnt
    _cnt += 1; _V_R2[_j] = _cnt
V_Z = lambda j: _V_Z[j]
V_R2 = lambda j: _V_R2[j]
_SORDER = []
for _j in range(NT - 1):
    _SORDER += [("o1", _j), ("sq", _j)]
_SORDER += [("o1", NT - 1), ("tp", 0), ("tp", 1), ("tp", 2), ("tp", 3),
            ("sq", NT - 1)]
_SORDER += [("tp", _j) for _j in range(4, 14)]
for _j in range(NT):
    _SORDER.append(("ln", _j))
    if _j + 14 < NT:
        _SORDER.append(("tp", _j + 14))
_SC_O1 = [0] * NT
_SC_SQ = [0] * NT
_SC_TP = [0] * NT
_SCD_SQ = [0] * NT
_SCD_OUT = [0] * NT
_cnt = SC0C
for _k, _j in _SORDER:
    if _k == "o1":
        _cnt += 1; _SC_O1[_j] = _cnt
    elif _k == "sq":
        _cnt += 1; _SC_SQ[_j] = _cnt
    elif _k == "tp":
        _cnt += 1; _SC_TP[_j] = _cnt
    else:
        _cnt += 1; _SCD_SQ[_j] = _cnt
        _cnt += 1; _SCD_OUT[_j] = _cnt
SCC_O1 = lambda j: _SC_O1[j]
SCC_SQ = lambda j: _SC_SQ[j]
SC_TP = lambda j: _SC_TP[j]
SCD_SQ = lambda j: _SCD_SQ[j]
SCD_OUT = lambda j: _SCD_OUT[j]
DLP_PACKT = 32
DGP_ALL = 64


def build_nc(debug=False):
    nc = bass.Bass("TRN2", target_bir_lowering=False)

    x_d = nc.dram_tensor("x", [S, D], F32, kind="ExternalInput")
    xT_d = nc.dram_tensor("xT", [D, S], F32, kind="ExternalInput")
    wqkT_d = nc.dram_tensor("wqkT", [D, 258], F32, kind="ExternalInput")
    bcols_d = nc.dram_tensor("bcols", [128, 3], F32, kind="ExternalInput")
    wvT_d = nc.dram_tensor("wvT", [D, D], F32, kind="ExternalInput")
    bvb_d = nc.dram_tensor("bvb", [128, D], F32, kind="ExternalInput")
    k0m_d = nc.dram_tensor("k0m", [128, 1], F32, kind="ExternalInput")
    k0e_d = nc.dram_tensor("k0e", [1, 1], F32, kind="ExternalInput")
    v0r_d = nc.dram_tensor("v0r", [1, D], F32, kind="ExternalInput")
    WT_d = nc.dram_tensor("WT", [D, D], F32, kind="ExternalInput")
    lpack_d = nc.dram_tensor("lpack", [128, 2112], F32, kind="ExternalInput")
    onesp_d = nc.dram_tensor("onesp", [128, S], BF16, kind="ExternalInput")
    browp_d = nc.dram_tensor("browp", [128, D], F32, kind="ExternalInput")
    idf_d = nc.dram_tensor("idf_in", [128, 128], F32, kind="ExternalInput")
    out_d = nc.dram_tensor("out", [S, D], F32, kind="ExternalOutput")
    if debug:
        dbg = {nm: nc.dram_tensor("dbg_" + nm, shp, F32, kind="ExternalOutput")
               for nm, shp in [("mcols", [128, NT]), ("ssink", [128, NT]),
                                ("qe2", [2, S]), ("ke2", [2, S]), ("ptsink", [1, S]),
                                ("qmt", [128, S]), ("kmt", [128, S]),
                                ("pt0", [128, S]), ("x1f", [128, NT * D]),
                                ("packtf", [32, 128]), ("vh0", [128, VW]),
                                ("weff0", [128, D]), ("x1t0", [128, S])]}

    es = contextlib.ExitStack()
    sb = lambda name, shape, dt: es.enter_context(nc.sbuf_tensor(name, shape, dt))

    with es:
        QmT = sb("QmT", [128, S], BF16)
        KmT = sb("KmT", [128, S], BF16)
        Qe2 = sb("Qe2", [128, S], BF16)
        Ke2 = sb("Ke2", [128, S], BF16)
        Vh = sb("Vh", [128, NT * VW], BF16)
        Vs = sb("Vs", [128, VW], BF16)
        PTsink = sb("PTsink", [128, S], BF16)
        k0mb = sb("k0mb", [128, 1], BF16)
        k0eb = sb("k0eb", [1, 1], BF16)
        ssink = sb("ssink", [128, NT], F32)
        mcols = sb("mcols", [128, NT], F32)
        negmb = sb("negmb", [128, NT], BF16)
        psd = sb("psd", [128, NT], F32)
        packf = sb("packf", [128, 32], F32)
        packTf = sb("packTf", [32, 128], BF16)
        idf = sb("idf", [128, 128], F32)
        X1f = sb("X1f", [128, NT * D], F32)
        xs = [sb(f"xs{i}", [128, D], F32) for i in range(2)]
        o1s = [sb(f"o1s{i}", [128, D], F32) for i in range(2)]
        st12 = [sb(f"st12{i}", [128, 12], F32) for i in range(2)]
        smalls = sb("smalls", [128, 16], F32)
        tmpm = smalls[:, 0:1]
        sigp = [smalls[:, 1 + i:2 + i] for i in range(2)]
        sigs = [smalls[:, 3 + i:4 + i] for i in range(2)]
        r1p = [smalls[:, 5 + i:6 + i] for i in range(2)]
        nmrp = [smalls[:, 7 + i:8 + i] for i in range(2)]
        rls = [smalls[:, 9 + i:10 + i] for i in range(2)]
        epsc = smalls[:, 11:12]
        modd = sb("modd", [128, NT], F32)
        mvp2 = sb("mvp2", [128, 4], F32)
        mvp = [mvp2[:, 2 * i:2 * i + 2] for i in range(2)]

        dg = es.enter_context(nc.semaphore("dg"))
        dsy = es.enter_context(nc.semaphore("dsy"))
        t = es.enter_context(nc.semaphore("t"))
        v = es.enter_context(nc.semaphore("v"))
        sc = es.enter_context(nc.semaphore("sc"))
        dgc = [es.enter_context(nc.semaphore(f"dgc{i}")) for i in range(ND)]
        dx = [es.enter_context(nc.semaphore(f"dx{i}")) for i in range(2)]
        dlp = es.enter_context(nc.semaphore("dlp"))
        dgp = es.enter_context(nc.semaphore("dgp"))
        dg0b = es.enter_context(nc.semaphore("dg0b"))

        AR = sb("AR", [128, 32768], BF16)
        wqkb = AR[:, 0:ND * 258]
        wvb = AR[:, 2064:2064 + ND * D]
        XT = AR[:, 10256:10256 + ND * S]
        PT = AR
        X1T = AR
        bvbs = X1f[:, 0:D]
        bcols = X1f[:, D:D + 3]
        ys = [sb(f"ys{i}", [128, D], F32) for i in range(2)]
        pq = es.enter_context(nc.psum_tensor("pq", [128, 2048], F32))
        pv = [es.enter_context(nc.psum_tensor(f"pv{i}", [128, 1024], F32)) for i in range(2)]
        PO = [pq[:, 0:1024], pq[:, 1024:2048]]
        PL = [pv[0][:, 0:512], pv[0][:, 512:1024]]
        TP0 = pv[1]
        FO = PO
        WeffT = Vh[:, 0:8192]
        Ab = Vh[0:4, 13376:14400]
        Bb = Vh[:, 14400:15424]
        WTs = xs
        dab = QmT[0:100, 0:1024]
        uab = QmT[:, 1024:2048]
        ldb = KmT[0:100, 0:4]
        lub = KmT[0:4, 4:54]
        bbr = KmT[:, 64:1088]
        onesS = Qe2[:, 0:2048]

        with nc.Block() as blk:
            @blk.gpsimd
            def _(g):
                for c in range(ND):
                    g.dma_start(wqkb[:, c * 258:(c + 1) * 258],
                                wqkT_d[c * 128:(c + 1) * 128, :]).then_inc(dgc[c], 16)
                    if c == 0:
                        g.dma_start(XT[:, 0:1024],
                                    xT_d[0:128, 0:1024]).then_inc(dgc[0], 16)
                        g.dma_start(XT[:, 1024:2048],
                                    xT_d[0:128, 1024:2048]).then_inc(dg0b, 16)
                    else:
                        g.dma_start(XT[:, c * S:(c + 1) * S],
                                    xT_d[c * 128:(c + 1) * 128, :]).then_inc(dgc[c], 16)
                for c in range(ND):
                    g.dma_start(wvb[:, c * D:(c + 1) * D],
                                wvT_d[c * 128:(c + 1) * 128, :]).then_inc(dg, 16)
                g.dma_start(k0mb[:, :], k0m_d[:, :]).then_inc(dg, 16)
                g.dma_start(k0eb[:, :], k0e_d[:, :]).then_inc(dg, 16)
                g.wait_ge(v, V_ONES)
                g.dma_start(Vs[0:1, 0:D], v0r_d[:, :]).then_inc(dg, 16)
                g.dma_start(Ke2[1:2, :], PTsink[0:1, :]).then_inc(dg, 16)
                g.wait_ge(sc, SC_E2)
                g.wait_ge(dg, DG_WV)
                g.dma_start(Ke2[0:1, :], Qe2[1:2, :]).then_inc(dg, 16)
                g.wait_ge(t, TB(NT - 1))
                g.dma_start(QmT[:, :], lpack_d[:, 0:2048]).then_inc(dgp, 16)
                g.dma_start(KmT[:, 0:64], lpack_d[:, 2048:2112]).then_inc(dgp, 16)
                g.dma_start(bbr, browp_d[:, :]).then_inc(dgp, 16)
                g.dma_start(onesS, onesp_d[:, :]).then_inc(dgp, 16)

            @blk.sync
            def _(sy):
                sy.dma_start(bvbs[:, :], bvb_d[:, :]).then_inc(dsy, 16)
                sy.dma_start(bcols[:, :], bcols_d[:, :]).then_inc(dsy, 16)
                sy.dma_start(idf[:, :], idf_d[:, :]).then_inc(dsy, 16)
                sy.wait_ge(v, V_PACKT)
                sy.dma_start(Qe2[1:2, :], packTf[0:16, :]).then_inc(dlp, 16)
                sy.dma_start(PTsink[0:1, :], packTf[16:32, :]).then_inc(dlp, 16)

                for j in range(NT):
                    if j >= 2:
                        sy.wait_ge(v, VC_Y(j - 2))
                    sy.dma_start(xs[j % 2][:, :],
                                 x_d[j * 128:(j + 1) * 128, :]).then_inc(dx[j % 2], 16)
                for c in range(ND):
                    if c == 0:
                        sy.wait_ge(v, VC_Y(NT - 2))
                    elif c == 1:
                        sy.wait_ge(v, VC_Y(NT - 1))
                    else:
                        sy.wait_ge(v, V_WEFF(c - 2))
                    sy.dma_start(WTs[c % 2][:, :],
                                 WT_d[c * 128:(c + 1) * 128, :]).then_inc(dx[c % 2], 16)
                for j in range(NT):
                    if j == NT - 1:
                        sy.wait_ge(sc, SCD_OUT(j))
                        sy.dma_start(out_d[j * 128:(j + 1) * 128, 0:512],
                                     o1s[j % 2][:, 0:512]).then_inc(dx[j % 2], 16)
                        sy.wait_ge(sc, SCD_OUT(j) + 1)
                        sy.dma_start(out_d[j * 128:(j + 1) * 128, 512:1024],
                                     o1s[j % 2][:, 512:1024]).then_inc(dx[j % 2], 16)
                    else:
                        sy.wait_ge(sc, SCD_OUT(j))
                        sy.dma_start(out_d[j * 128:(j + 1) * 128, :],
                                     o1s[j % 2][:, :]).then_inc(dx[j % 2], 16)
                sy.wait_ge(dx[0], DX_FINAL)
                sy.wait_ge(dx[1], DX_FINAL + 16)

            @blk.tensor
            def _(te):
                for c in range(ND):
                    te.wait_ge(dgc[c], 32)
                    for n in range(NC4):
                        if c == 0 and n == 2:
                            te.wait_ge(dg0b, 16)
                        n0, n1 = n * 512, (n + 1) * 512
                        te.matmul(pq[:, n0:n1],
                                  wqkb[:, c * 258: c * 258 + 128],
                                  XT[:, c * S + n0: c * S + n1],
                                  start=(c == 0), stop=(c == ND - 1))
                    for n in range(NC4):
                        n0, n1 = n * 512, (n + 1) * 512
                        outp = (pv[n // 2])[:, (n % 2) * 512:(n % 2 + 1) * 512]
                        te.matmul(outp,
                                  wqkb[:, c * 258 + 128: c * 258 + 256],
                                  XT[:, c * S + n0: c * S + n1],
                                  start=(c == 0), stop=(c == ND - 1))
                te.drain(fusable=True).then_inc(t, 1)
                te.drain(fusable=True).then_inc(t, 1)
                te.wait_ge(sc, SC_QM)
                for n in range(NC4):
                    n0, n1 = n * 512, (n + 1) * 512
                    for c in range(ND):
                        ins = te.matmul(pq[0:2, n0:n1],
                                        wqkb[:, c * 258 + 256: c * 258 + 258],
                                        XT[:, c * S + n0: c * S + n1],
                                        start=(c == 0), stop=(c == ND - 1))
                te.drain(fusable=True).then_inc(t, 1)
                for tt in range(NT):
                    if tt < 2:
                        te.wait_ge(sc, SC_KM)
                    else:
                        te.wait_ge(v, V_VE0 + (tt - 2) + 1)
                    for n in range(2):
                        n0, n1 = n * 512, (n + 1) * 512
                        for c in range(ND):
                            if tt == 0 and n == 0 and c == 0:
                                te.wait_ge(dg, DG_WV)
                            ins = te.matmul(pv[tt % 2][:, n0:n1],
                                            XT[:, c * S + tt * 128: c * S + (tt + 1) * 128],
                                            wvb[:, c * D + n0: c * D + n1],
                                            start=(c == 0), stop=(c == ND - 1))
                    te.drain(fusable=True).then_inc(t, 1)
                te.wait_ge(sc, SC_E2)
                te.wait_ge(dg, DG_E2SPLIT)
                for j in range(NT):
                    te.matmul(pq[:, j:j + 1], QmT[:, j * 128:(j + 1) * 128],
                              k0mb[:, :], start=True, stop=False)
                    ins = te.matmul(pq[:, j:j + 1], Qe2[0:1, j * 128:(j + 1) * 128],
                                    k0eb[:, :], start=False, stop=True)
                te.drain(fusable=True).then_inc(t, 1)
                for j in range(NT):
                    if j == 0:
                        te.wait_ge(v, V_SSINK)
                    elif j == 1:
                        te.wait_ge(v, V_VE0 + NT)
                    else:
                        te.wait_ge(v, V_M0 + (j - 2) + 1)
                    for n in range(NC4):
                        n0, n1 = n * 512, (n + 1) * 512
                        if j % 2 == 0:
                            outp = pq[:, n0:n1]
                        else:
                            outp = (pv[n // 2])[:, (n % 2) * 512:(n % 2 + 1) * 512]
                        te.matmul(outp, QmT[:, j * 128:(j + 1) * 128],
                                  KmT[:, n0:n1], start=True, stop=False)
                        te.matmul(outp, Qe2[:, j * 128:(j + 1) * 128],
                                  Ke2[:, n0:n1], start=False, stop=True)
                    te.drain(fusable=True).then_inc(t, 1)
                te.wait_ge(sc, SC_PSINK)
                te.wait_ge(dsy, DSY_PRO)
                te.matmul(pv[0][0:32, 0:128], packf[:, :], idf[:, :],
                          is_transpose=True, start=True, stop=True)
                te.drain(fusable=True).then_inc(t, 1)

                def sb5(tt, n):
                    if tt % 2 == 0:
                        return pq[:, n * 512:(n + 1) * 512]
                    return (pv[n // 2])[:, (n % 2) * 512:(n % 2 + 1) * 512]
                te.wait_ge(v, V_M0 + 15)
                for n in range(NC4):
                    te.matmul(sb5(0, n), KmT[:, 0:128],
                              QmT[:, n * 512:(n + 1) * 512], start=True, stop=False)
                te.wait_ge(v, V_PACKT)
                for n in range(NC4):
                    te.matmul(sb5(1, n), KmT[:, 128:256],
                              QmT[:, n * 512:(n + 1) * 512], start=True, stop=False)
                te.wait_ge(dlp, DLP_PACKT)
                for tt in range(NT):
                    if tt >= 2:
                        te.wait_ge(sc, SCB(tt - 2))
                    for n in range(NC4):
                        n0, n1 = n * 512, (n + 1) * 512
                        if tt >= 2:
                            te.matmul(sb5(tt, n),
                                      KmT[:, tt * 128:(tt + 1) * 128],
                                      QmT[:, n0:n1], start=True, stop=False)
                        te.matmul(sb5(tt, n),
                                  Ke2[:, tt * 128:(tt + 1) * 128],
                                  Qe2[:, n0:n1], start=False, stop=True)
                    te.drain(fusable=True).then_inc(t, 1)

                te.wait_ge(dlp, DLP_PACKT)
                te.wait_ge(v, V_PACKT)
                for j in range(NT):
                    if j == 0:
                        te.wait_ge(sc, SCB15A)
                    elif j == 1:
                        te.wait_ge(sc, SCB(14))
                    if j >= 2:
                        te.wait_ge(sc, SCC_O1(j - 2))
                        te.wait_ge(v, VC_RL(j - 2))
                    for c in range(NT + 1):
                        first, last = (c == 0), (c == NT)
                        if c < NT:
                            lhs = PT[:, c * S + j * 128: c * S + (j + 1) * 128]
                            r1_ = Vh[:, c * VW: c * VW + 512]
                            r2_ = Vh[:, c * VW + 512: c * VW + 1024]
                            r3_ = Vh[:, c * VW + D: c * VW + VW]
                        else:
                            lhs = PTsink[:, j * 128:(j + 1) * 128]
                            r1_ = Vs[:, 0:512]
                            r2_ = Vs[:, 512:1024]
                            r3_ = Vs[:, D:VW]
                        te.matmul(PL[j % 2][:, 0:1], lhs, r3_, start=first, stop=last)
                        te.matmul(PO[j % 2][:, 0:512], lhs, r1_, start=first, stop=last)
                        te.matmul(PO[j % 2][:, 512:1024], lhs, r2_, start=first, stop=last)
                    te.drain(fusable=True).then_inc(t, 1)

                def emit_A():
                    te.wait_ge(dgp, DGP_ALL)
                    te.wait_ge(sc, SCC_O1(NT - 2))
                    for n in range(2):
                        te.matmul(FO[0][0:4, n * 512:(n + 1) * 512], ldb,
                                  dab[:, n * 512:(n + 1) * 512], start=True, stop=True)
                    te.drain(fusable=True).then_inc(t, 1)

                def emit_B():
                    te.wait_ge(v, V_A)
                    for n in range(2):
                        te.matmul(FO[0][0:50, n * 512:(n + 1) * 512], lub,
                                  Ab[:, n * 512:(n + 1) * 512], start=True, stop=True)
                    te.drain(fusable=True).then_inc(t, 1)

                def emit_ct(c):
                    if c == 1:
                        te.wait_ge(sc, SCC_O1(NT - 1))
                    te.wait_ge(v, V_B if c < 2 else V_WEFF(c - 2))
                    for n in range(2):
                        te.matmul(FO[c % 2][:, n * 512:(n + 1) * 512],
                                  Bb[:, c * 128:(c + 1) * 128],
                                  uab[:, n * 512:(n + 1) * 512], start=True, stop=True)
                    te.drain(fusable=True).then_inc(t, 1)

                def emit_tp(j):
                    te.wait_ge(v, VC_R1(j))
                    if j == 1:
                        te.wait_ge(v, VC_RL(NT - 1))
                    if j >= 2:
                        te.wait_ge(sc, SC_TP(j - 2))
                    for b in range(ND):
                        if j % 2 == 0:
                            outp = TP0[:, b * 128:(b + 1) * 128]
                        else:
                            outp = PL[b // 4][:, (b % 4) * 128:(b % 4 + 1) * 128]
                        te.matmul(outp,
                                  X1f[:, j * D + b * 128: j * D + (b + 1) * 128],
                                  idf[:, :], is_transpose=True,
                                  start=(b % 4 == 0), stop=(b % 4 == 3))
                    te.drain(fusable=True).then_inc(t, 1)

                def emit_ffn(j):
                    if j == 0:
                        te.wait_ge(dgp, DGP_ALL)
                    te.wait_ge(sc, SC_TP(j))
                    if j < 2:
                        te.wait_ge(v, V_WEFF(ND - 1))
                    else:
                        te.wait_ge(v, V_Z(j - 2))
                    for n in range(2):
                        n0, n1 = n * 512, (n + 1) * 512
                        for c in range(ND):
                            te.matmul(FO[j % 2][:, n0:n1],
                                      X1T[:, c * S + j * 128: c * S + (j + 1) * 128],
                                      WeffT[:, c * D + n0: c * D + n1],
                                      start=(c == 0), stop=False)
                        te.matmul(FO[j % 2][:, n0:n1],
                                  onesS[:, j * 128:(j + 1) * 128],
                                  bbr[:, n0:n1], start=False, stop=True)
                        if j == NT - 1 and n == 0:
                            te.drain(fusable=True).then_inc(t, 1)
                    te.drain(fusable=True).then_inc(t, 1)

                emitters = {"tp": emit_tp, "ffn": emit_ffn, "ct": emit_ct,
                            "A": lambda j: emit_A(), "B": lambda j: emit_B()}
                for k, j in _TORDER:
                    emitters[k](j)

            def emit_z(ve, j):
                p = j % 2
                if j == NT - 1:
                    ve.wait_ge(t, T_FOH15)
                    ve.tensor_tensor(xs[p][:, 0:512], X1f[:, j * D: j * D + 512],
                                     FO[p][:, 0:512], OP.add)
                    ve.drain()
                    ve.bn_stats(st12[p][:, 0:6], xs[p][:, 0:512])
                    ve.wait_ge(t, T_FO(j))
                    ve.tensor_tensor(xs[p][:, 512:1024],
                                     X1f[:, j * D + 512:(j + 1) * D],
                                     FO[p][:, 512:1024], OP.add)
                    ve.drain()
                    ve.bn_stats(st12[p][:, 6:12], xs[p][:, 512:1024])
                    ve.drain()
                else:
                    ve.wait_ge(t, T_FO(j))
                    ve.tensor_tensor(xs[p][:, 0:512], X1f[:, j * D: j * D + 512],
                                     FO[p][:, 0:512], OP.add)
                    ve.tensor_tensor(xs[p][:, 512:1024],
                                     X1f[:, j * D + 512:(j + 1) * D],
                                     FO[p][:, 512:1024], OP.add)
                    ve.drain()
                    ve.bn_stats(st12[p][:, 0:6], xs[p][:, 0:512])
                    ve.bn_stats(st12[p][:, 6:12], xs[p][:, 512:1024])
                    ve.drain()
                ve.bn_aggr(mvp[p], st12[p][:, :])
                ve.drain().then_inc(v, 1)
                ve.wait_ge(sc, SCD_SQ(j))
                ve.reciprocal(r1p[p], sigs[p])
                ve.drain()
                ve.tensor_scalar(nmrp[p], mvp[p][:, 0:1], r1p[p],
                                 -1.0, OP.mult, OP.mult)
                ve.drain().then_inc(v, 1)

            @blk.vector
            def _(ve):
                ve.memset(epsc, EPS)
                ve.memset(Qe2[:, :], 0.0)
                ve.memset(Ke2[:, :], 0.0)
                ve.memset(PTsink[:, :], 0.0)
                ve.memset(Vs[:, :], 0.0)
                ve.drain()
                ve.memset(PTsink[0:1, :], 1.0)
                ve.memset(Vs[0:1, D:VW], 1.0).then_inc(v, 1)
                ve.memset(bass.AP(Vh, D, [[NT * VW, 128], [VW, NT]]), 1.0).then_inc(v, 1)
                ve.wait_ge(dsy, DSY_PRO)
                for tt in range(NT):
                    ve.wait_ge(t, T_VP0 + tt + 1)
                    ve.tensor_tensor(Vh[:, tt * VW: tt * VW + D], pv[tt % 2][:, :],
                                     bvbs[:, :], OP.add).then_inc(v, 1)
                ve.wait_ge(t, T_PSK)
                ve.tensor_copy(ssink[:, :], pq[:, 0:NT]).then_inc(v, 1)
                for j in range(NT):
                    ve.wait_ge(t, T_SA0 + j + 1)
                    if j % 2 == 0:
                        ve.reduce_max(mcols[:, j:j + 1], pq[:, :], axis=AX).then_inc(v, 1)
                    else:
                        jo = j // 2
                        ve.reduce_max(modd[:, 2 * jo:2 * jo + 1], pv[0][:, :], axis=AX)
                        ve.reduce_max(modd[:, 2 * jo + 1:2 * jo + 2], pv[1][:, :],
                                      axis=AX).then_inc(v, 1)
                ve.drain()
                ve.tensor_tensor(bass.AP(mcols, 1, [[NT, 128], [2, 8]]),
                                 bass.AP(modd, 0, [[NT, 128], [2, 8]]),
                                 bass.AP(modd, 1, [[NT, 128], [2, 8]]), OP.max)
                ve.drain()
                ve.tensor_tensor(mcols[:, :], mcols[:, :], ssink[:, :], OP.max)
                ve.drain()
                ve.tensor_scalar_mul(negmb[:, :], mcols[:, :], -1.0)
                ve.drain()
                ve.tensor_tensor(psd[:, :], ssink[:, :], negmb[:, :], OP.add)
                ve.tensor_copy(packf[:, 0:16], negmb[:, :])
                ve.drain().then_inc(v, 1)
                ve.wait_ge(t, T_PACKT)
                ve.tensor_copy(packTf[:, :], pv[0][0:32, 0:128]).then_inc(v, 1)

                def p6a_head(j):
                    p = j % 2
                    ve.wait_ge(t, TC(j))
                    ve.reciprocal(rls[p], PL[p][:, 0:1])
                    ve.drain().then_inc(v, 1)

                def p6a_ln(j):
                    p = j % 2
                    ve.wait_ge(sc, SCC_O1(j))
                    ve.wait_ge(dx[j % 2], DX_X(j))
                    ve.scalar_tensor_tensor(ys[p][:, 0:512], xs[p][:, 0:512], 2.0,
                                            o1s[p][:, 0:512], OP.mult, OP.add)
                    ve.scalar_tensor_tensor(ys[p][:, 512:1024], xs[p][:, 512:1024], 2.0,
                                            o1s[p][:, 512:1024], OP.mult, OP.add)
                    ve.drain()
                    ve.bn_stats(st12[p][:, 0:6], ys[p][:, 0:512])
                    ve.bn_stats(st12[p][:, 6:12], ys[p][:, 512:1024])
                    ve.drain()
                    ve.bn_aggr(mvp[p], st12[p][:, :])
                    ve.drain().then_inc(v, 1)
                    ve.wait_ge(sc, SCC_SQ(j))
                    ve.reciprocal(r1p[p], sigs[p])
                    ve.drain()
                    ve.tensor_scalar(nmrp[p], mvp[p][:, 0:1], r1p[p],
                                     -1.0, OP.mult, OP.mult)
                    ve.drain()
                    ve.tensor_scalar(X1f[:, j * D:(j + 1) * D], ys[p][:, :],
                                     r1p[p], nmrp[p], OP.mult, OP.add)
                    ve.drain().then_inc(v, 1)

                for j in range(NT - 1):
                    p6a_head(j)
                    p6a_ln(j)
                p6a_head(NT - 1)
                ve.memset(Bb, 0.0)
                ve.wait_ge(t, T_A)
                ve.tensor_copy(Ab, FO[0][0:4, :])
                ve.drain().then_inc(v, 1)
                ve.wait_ge(t, T_B)
                ve.tensor_copy(Bb[0:50, :], FO[0][0:50, :])
                ve.drain().then_inc(v, 1)
                p6a_ln(NT - 1)
                for c in range(ND):
                    ve.wait_ge(t, T_CT(c))
                    ve.wait_ge(dx[c % 2], DX_WT(c))
                    ve.tensor_tensor(WeffT[:, c * D:(c + 1) * D], FO[c % 2][:, :],
                                     WTs[c % 2][:, :], OP.add)
                    ve.drain().then_inc(v, 1)
                for j in range(NT):
                    emit_z(ve, j)

            @blk.scalar
            def _(s):
                s.wait_ge(v, V_ONES)
                s.activation(o1s[0][:, 0:1], epsc, AF.Identity,
                             bias=epsc, scale=1.0)
                s.activation(o1s[1][:, 0:1], epsc, AF.Copy)
                s.activation(ys[0][:, 0:1], epsc, AF.Exp)
                s.activation(ys[1][:, 0:1], epsc, AF.Sqrt, bias=epsc, scale=1.0)
                s.wait_ge(t, T_QM)
                s.wait_ge(v, V_ONES)
                s.wait_ge(dsy, DSY_PRO)
                s.activation(QmT[:, :], pq[:, :], AF.Identity,
                             bias=bcols[:, 0:1], scale=1.0).then_inc(sc, 1)
                s.wait_ge(t, T_KM)
                s.activation(KmT[:, 0:1024], pv[0][:, :], AF.Identity,
                             bias=bcols[:, 1:2], scale=1.0)
                s.activation(KmT[:, 1024:2048], pv[1][:, :], AF.Identity,
                             bias=bcols[:, 1:2], scale=1.0).then_inc(sc, 1)
                s.wait_ge(t, T_E2)
                s.activation(Qe2[0:2, :], pq[0:2, :], AF.Identity,
                             bias=bcols[0:2, 2:3], scale=1.0).then_inc(sc, 1)
                s.wait_ge(v, V_PACK)
                s.activation(packf[:, 16:32], psd[:, :], AF.Exp).then_inc(sc, 1)

                for tt in range(NT - 2):
                    s.wait_ge(t, TB(tt))
                    if tt % 2 == 0:
                        s.activation(PT[:, tt * S:(tt + 1) * S], pq[:, :],
                                     AF.Exp).then_inc(sc, 1)
                    else:
                        s.activation(PT[:, tt * S: tt * S + 1024], pv[0][:, :], AF.Exp)
                        s.activation(PT[:, tt * S + 1024:(tt + 1) * S], pv[1][:, :],
                                     AF.Exp).then_inc(sc, 1)
                s.wait_ge(t, TB(14))
                s.activation(PT[:, 14 * S: 14 * S + 1024], pq[:, 0:1024],
                             AF.Exp).then_inc(sc, 1)
                s.wait_ge(t, TB(15))
                s.activation(PT[:, 15 * S: 15 * S + 1024], pv[0][:, :],
                             AF.Exp).then_inc(sc, 1)
                s.activation(PT[:, 14 * S + 1024: 15 * S], pq[:, 1024:2048],
                             AF.Exp).then_inc(sc, 1)
                s.activation(PT[:, 15 * S + 1024:(15 + 1) * S], pv[1][:, :],
                             AF.Exp).then_inc(sc, 1)

                def s_o1(j):
                    p = j % 2
                    s.wait_ge(v, VC_RL(j))
                    s.activation(o1s[p][:, :], PO[p][:, :], AF.Identity,
                                 scale=rls[p]).then_inc(sc, 1)
                def s_sq(j):
                    p = j % 2
                    s.wait_ge(v, VC_Y(j))
                    s.activation(sigs[p], mvp[p][:, 1:2], AF.Sqrt,
                                 bias=epsc, scale=1.0).then_inc(sc, 1)
                def s_tp(j):
                    s.wait_ge(t, T_TP(j))
                    if j % 2 == 0:
                        s.activation(
                            bass.AP(PT, j * 128, [[NT * S, 128], [S, ND], [1, 128]]),
                            TP0[:, :], AF.Copy).then_inc(sc, 1)
                    else:
                        s.activation(
                            bass.AP(PT, j * 128, [[NT * S, 128], [S, 4], [1, 128]]),
                            PL[0][:, :], AF.Copy)
                        s.activation(
                            bass.AP(PT, 4 * S + j * 128,
                                    [[NT * S, 128], [S, 4], [1, 128]]),
                            PL[1][:, :], AF.Copy).then_inc(sc, 1)

                def s_ln(j):
                    p = j % 2
                    s.wait_ge(v, V_Z(j))
                    s.activation(sigs[p], mvp[p][:, 1:2], AF.Sqrt,
                                 bias=epsc, scale=1.0).then_inc(sc, 1)
                    s.wait_ge(v, V_R2(j))
                    if j >= 2:
                        s.wait_ge(dx[j % 2], DX_OUT(j - 2))
                    if j == NT - 1:
                        s.activation(o1s[p][:, 0:512], xs[p][:, 0:512], AF.Identity,
                                     bias=nmrp[p], scale=r1p[p]).then_inc(sc, 1)
                        s.activation(o1s[p][:, 512:1024], xs[p][:, 512:1024],
                                     AF.Identity,
                                     bias=nmrp[p], scale=r1p[p]).then_inc(sc, 1)
                    else:
                        s.activation(o1s[p][:, :], xs[p][:, :], AF.Identity,
                                     bias=nmrp[p], scale=r1p[p]).then_inc(sc, 1)

                sdisp = {"o1": s_o1, "sq": s_sq, "tp": s_tp, "ln": s_ln}
                for k, j in _SORDER:
                    sdisp[k](j)


    return nc


_NC_CACHE = {}


def kernel(**inp):
    x = np.ascontiguousarray(np.asarray(inp["L0_attn_x"], dtype=np.float32))
    wq = np.asarray(inp["L0_attn_w_q"], dtype=np.float32)
    bq = np.asarray(inp["L0_attn_b_q"], dtype=np.float32)
    wk = np.asarray(inp["L0_attn_w_k"], dtype=np.float32)
    bk = np.asarray(inp["L0_attn_b_k"], dtype=np.float32)
    wv = np.asarray(inp["L0_attn_w_v"], dtype=np.float32)
    bv = np.asarray(inp["L0_attn_b_v"], dtype=np.float32)
    k0 = np.asarray(inp["L0_attn_k_0"], dtype=np.float32)
    v0 = np.asarray(inp["L0_attn_v_0"], dtype=np.float32)
    W = np.asarray(inp["L0_ffn_W"], dtype=np.float32)
    b = np.asarray(inp["L0_ffn_b"], dtype=np.float32)
    ld = np.asarray(inp["L0_ffn_lora_down"], dtype=np.float32)
    lu = np.asarray(inp["L0_ffn_lora_up"], dtype=np.float32)
    da = np.asarray(inp["L0_ffn_down_aux"], dtype=np.float32)
    ua = np.asarray(inp["L0_ffn_up_aux"], dtype=np.float32)

    wqkT = np.ascontiguousarray(
        np.concatenate([wq[:128].T, wk[:128].T, wq[128:129].T, wk[128:129].T], axis=1))
    lpack = np.zeros((128, 2112), dtype=np.float32)
    lpack[0:100, 0:1024] = da
    lpack[0:50, 1024:2048] = ua.T
    lpack[0:100, 2048:2052] = ld.T
    lpack[0:4, 2052:2102] = lu.T
    browp = np.zeros((128, 1024), dtype=np.float32)
    browp[0] = b
    onesp = np.zeros((128, 2048), dtype=ml_dtypes.bfloat16)
    onesp[0] = 1.0
    bcols = np.zeros((128, 3), dtype=np.float32)
    bcols[:, 0] = bq[:128]
    bcols[:, 1] = bk[:128]
    bcols[0, 2] = bq[128]
    bcols[1, 2] = bk[128]

    common = dict(
        wqkT=wqkT, bcols=bcols,
        wvT=np.ascontiguousarray(wv.T),
        bvb=np.ascontiguousarray(np.tile(bv[None, :], (128, 1))),
        k0m=np.ascontiguousarray(k0[:128, None]),
        k0e=np.ascontiguousarray(k0[128:129, None]),
        v0r=np.ascontiguousarray(v0[None, :]),
        WT=np.ascontiguousarray(W.T),
        browp=browp, onesp=onesp,
        lpack=lpack,
        idf_in=np.eye(128, dtype=np.float32),
    )

    if "nc" not in _NC_CACHE:
        _NC_CACHE["nc"] = build_nc()
    nc = _NC_CACHE["nc"]

    in_maps = []
    for core in range(8):
        m = dict(common)
        m["x"] = np.ascontiguousarray(x[core])
        m["xT"] = np.ascontiguousarray(x[core].T)
        in_maps.append(m)

    res = run_bass_kernel_spmd(nc, in_maps, core_ids=list(range(8)))
    out = np.stack([np.asarray(res.results[core]["out"]) for core in range(8)], axis=0)
    return out.astype(np.float32)
